# version 08
# baseline (speedup 1.0000x reference)
import contextlib
import numpy as np
import ml_dtypes
import concourse.bass as bass
import concourse.mybir as mybir
from concourse.bass_utils import run_bass_kernel_spmd

F32 = mybir.dt.float32
I32 = mybir.dt.int32
BF16 = mybir.dt.bfloat16
AF = mybir.ActivationFunctionType
ALU = mybir.AluOpType
AX = mybir.AxisListType

HID = 2048
INTER = 4096
NH, HD, N = 64, 64, 128
CONV = 4352
PROJ = 8512
T = 1024
TE = T + 3
NKC = HID // 128
NTT = T // 128
NPAN = 17
C_MAGIC = float(np.float32(3 * 2 ** 22))
INV_DM = 1.0 / 2048.0
INV_INTER = 1.0 / 4096.0

ENGINES = ("tensor", "vector", "scalar", "gpsimd", "sync")
USE_COLLECTIVE = True
RG = [[0, 1], [2, 3], [4, 5], [6, 7]]


K_DMA = 8


class _Op:
    __slots__ = ("eng", "fn", "reads", "writes", "idx", "is_dma", "milestone",
                 "waits", "val_after", "dep_ops", "tag", "semkey")


class Prog:

    def __init__(self):
        self.ops = []
        self.last_write = {}
        self.readers = {}
        self.fences = []
        self.dma_count = {}

    def _add(self, eng, fn, reads, writes, is_dma, tag):
        op = _Op()
        op.eng, op.fn = eng, fn
        op.reads, op.writes = tuple(reads), tuple(writes)
        op.is_dma = is_dma
        op.milestone = is_dma
        op.waits = []
        op.tag = tag
        op.idx = len(self.ops)
        if is_dma:
            n = self.dma_count.get(eng, 0)
            self.dma_count[eng] = n + 1
            op.semkey = (eng, n % K_DMA)
        else:
            op.semkey = eng
        deps = {}
        for r in op.reads:
            w = self.last_write.get(r)
            if w is not None:
                deps[w.idx] = w
        for w_ in op.writes:
            lw = self.last_write.get(w_)
            if lw is not None:
                deps[lw.idx] = lw
            for rd in self.readers.get(w_, ()):
                deps[rd.idx] = rd
        op.dep_ops = [d for d in deps.values() if d.idx != op.idx
                      and not (d.eng == "tensor" and eng == "tensor")]
        for d in op.dep_ops:
            d.milestone = True
        self.ops.append(op)
        for r in op.reads:
            self.readers.setdefault(r, []).append(op)
        for w_ in op.writes:
            self.last_write[w_] = op
            self.readers[w_] = []
        return op

    def t(self, fn, reads, writes, tag=""):
        return self._add("tensor", fn, reads, writes, False, tag)

    def v(self, fn, reads, writes, tag=""):
        return self._add("vector", fn, reads, writes, False, tag)

    def s(self, fn, reads, writes, tag=""):
        return self._add("scalar", fn, reads, writes, False, tag)

    def pool(self, fn, reads, writes, tag=""):
        return self._add("gpsimd", fn, reads, writes, False, tag)

    def dma(self, q, fn, reads, writes, tag=""):
        return self._add(q, fn, reads, writes, True, tag)

    def cc(self, q, fn, reads, writes, tag=""):
        op = self._add(q, fn, reads, writes, True, tag)
        self.dma_count[q] -= 1
        op.semkey = (q, "cc")
        return op

    def fence(self):
        last = {}
        for o in self.ops:
            last[o.semkey] = o
        for o in last.values():
            o.milestone = True
        self.fences.append((len(self.ops), dict(last)))

    def resolve(self):
        cnt = {}
        for op in self.ops:
            if op.milestone:
                cnt[op.semkey] = cnt.get(op.semkey, 0) + (16 if op.is_dma else 1)
            op.val_after = cnt.get(op.semkey, 0)
        seen = {}
        fence_ptr = {e: 0 for e in ENGINES}
        for op in self.ops:
            need = {}
            fp = fence_ptr[op.eng]
            while fp < len(self.fences) and self.fences[fp][0] <= op.idx:
                for d in self.fences[fp][1].values():
                    if d.eng != op.eng or d.is_dma:
                        need[d.semkey] = max(need.get(d.semkey, 0), d.val_after)
                fp += 1
            fence_ptr[op.eng] = fp
            for d in op.dep_ops:
                need[d.semkey] = max(need.get(d.semkey, 0), d.val_after)
            if op.is_dma and op.val_after > 16:
                need[op.semkey] = max(need.get(op.semkey, 0), op.val_after - 16)
            for sk in sorted(need, key=repr):
                v = need[sk]
                if v > seen.get((op.eng, sk), 0):
                    op.waits.append((sk, v))
                    seen[(op.eng, sk)] = v

    def emit(self, nc):
        self.resolve()
        semkeys = {op.semkey for op in self.ops if op.milestone}
        for op in self.ops:
            for sk, _ in op.waits:
                semkeys.add(sk)
        sems = {}
        for sk in sorted(semkeys, key=repr):
            nm = sk if isinstance(sk, str) else f"{sk[0]}_d{sk[1]}"
            sems[sk] = nc.semaphore("sem_" + nm).__enter__()
        per = {e: [o for o in self.ops if o.eng == e] for e in ENGINES}

        def make_runner(ename):
            def run(eng):
                for op in per[ename]:
                    for (sk, v) in op.waits:
                        eng.wait_ge(sems[sk], v)
                    inst = op.fn(eng)
                    if op.milestone:
                        inst.then_inc(sems[op.semkey], 16 if op.is_dma else 1)
            return run

        with nc.Block() as block:
            block.tensor(make_runner("tensor"))
            block.vector(make_runner("vector"))
            block.scalar(make_runner("scalar"))
            block.gpsimd(make_runner("gpsimd"))
            block.sync(make_runner("sync"))


def build_nc(taps=()):
    taps = set(taps)
    nc = bass.Bass("TRN2", target_bir_lowering=False)
    p = Prog()
    es = contextlib.ExitStack()
    ec = es.enter_context

    def sb(name, shape, dt):
        return ec(nc.sbuf_tensor(name, shape, dt))

    def di(name, shape):
        return nc.dram_tensor(name, list(shape), F32, kind="ExternalInput").ap()

    x_fm_d = di("x_fm", [HID, TE])
    x_prev_d = di("x_prev", [HID, TE])
    x_tok_d = di("x_tok", [T, HID])
    wip_d = di("wip", [NPAN, NKC, 128, 512])
    convd_d = di("convd", [34, 4, 128, 128])
    convb_d = di("convb", [128, 34])
    wop_d = di("wop", [4, 32, 128, 512])
    wg_d = di("wg", [16, NKC, 128, 512])
    wu_d = di("wu", [16, NKC, 128, 512])
    wd_d = di("wd", [4, 64, 128, 512])
    dtb_d = di("dtb_t", [1, 512])
    alog_d = di("alog_t", [1, 512])
    drow_d = di("d_row", [1, 64])
    nw_d = di("nw_t", [128, 32])
    selm_d = di("selm", [128, 1])
    ident_d = di("ident_in", [128, 128])
    cumut_d = di("cumut_in", [128, 128])
    e01_d = di("e01_in", [64, INTER])
    slo4_d = di("slo4_in", [128, 512])
    negid_d = di("negid_in", [128, 128])

    out_d = nc.dram_tensor("out", [HID, T], F32, kind="ExternalOutput").ap()
    tap_d = {}

    def tapt(name, shape, dt=F32):
        if name in taps and name not in tap_d:
            tap_d[name] = nc.dram_tensor("tap_" + name, list(shape), dt,
                                         kind="ExternalOutput").ap()
        return tap_d.get(name)

    gate_dram = nc.dram_tensor("gate_dram", [8, NTT, 128, 512], BF16, kind="Internal").ap()
    flat_dram = nc.dram_tensor("flat_dram", [NTT, 6, 64, 128], BF16, kind="Internal").ap()
    mix_dram = nc.dram_tensor("mix_dram", [NTT, 128, HID], F32, kind="Internal").ap()
    part_dram = nc.dram_tensor("part_dram", [16, 128, T], F32, kind="Internal").ap()
    sinit_dram = nc.dram_tensor("sinit_dram", [N, INTER], F32, kind="Internal").ap()

    ident = sb("ident", [128, 128], BF16)
    slo4 = sb("slo4", [128, 512], BF16)
    negid = sb("negid", [128, 128], BF16)
    cumut = sb("cumut", [128, 128], F32)
    mask01 = sb("mask01", [128, 128], BF16)
    ones_col = sb("ones_col", [128, 1], F32)
    ones_r32 = sb("ones_r32", [1, 128], F32)
    ones_rb = sb("ones_rb", [1, 128], BF16)
    eps5 = sb("eps5", [128, 1], F32)
    eps8 = sb("eps8", [128, 1], F32)
    selm = sb("selm_sb", [128, 1], F32)
    nw_sb = sb("nw_sb", [128, 32], F32)
    convb_sb = sb("convb_sb", [128, 34], F32)
    B_fm = sb("B_fm", [128, T], BF16)
    C_fm = sb("C_fm", [128, T], BF16)
    BT = sb("BT", [128, T], BF16)
    w_fm = sb("w_fm", [64, NTT * 128], BF16)
    ea_fm = sb("ea_fm", [64, NTT * 128], BF16)
    lam_cr = sb("lam_cr", [64, NTT * 128], BF16)
    d_cr = sb("d_cr", [64, 128], BF16)
    lndt = sb("lndt", [128, 512], F32)
    stat = sb("stat", [128, 16], F32)
    sums8 = sb("sums8", [128, 8], F32)

    XT = sb("XT", [128, NTT * INTER], BF16)
    AR2 = sb("AR2", [128, 16448], BF16)
    AR3 = sb("AR3", [128, 16384], BF16)
    AR4 = sb("AR4", [128, 16384], BF16)
    AR5 = sb("AR5", [128, 14464], BF16)

    def cv(arena, off_b, width, dt):
        if dt == BF16:
            return arena[:, off_b // 2: off_b // 2 + width]
        return arena[:, off_b // 2: off_b // 2 + width * 2].bitcast(F32)

    x_fm_v = cv(AR2, 0, NKC * TE, BF16)
    wpan = [cv(AR3, 0, 8192, BF16), cv(AR3, 16384, 8192, BF16)]
    hbc_v = cv(AR4, 0, 2 * TE, BF16)
    hs_v = cv(AR4, 4160, T, BF16)
    gate_st = cv(AR4, 6208, 1024, BF16)
    convd_v = [cv(AR4, 8256, 512, BF16), cv(AR4, 9280, 512, BF16)]
    dt_tok = cv(AR5, 0, 512, F32)
    sp_dt = cv(AR5, 2048, 512, F32)
    ahat = cv(AR5, 4096, 512, F32)
    t_ab = cv(AR5, 6144, 512, F32)
    dta = cv(AR5, 8192, 512, F32)
    wsp = cv(AR5, 10240, 512, F32)
    hilo_off = 12288
    hilo = {k: cv(AR5, hilo_off + i * 1024, 512, BF16)
            for i, k in enumerate(("hi", "lo", "hi_n", "lo_n", "lnhi", "lnlo"))}
    stage64 = cv(AR5, 18432, 128, BF16)
    wsb_bf = cv(AR5, 18688, 512, BF16)
    lam_row = cv(AR5, 19712, 512, BF16)[0:1, :]
    dtb_bc = cv(AR5, 20736, 512, F32)
    arep_bc = cv(AR5, 22784, 512, F32)
    rbuf = cv(AR5, 24832, 512, F32)[0:1, :]
    arow = cv(AR5, 26880, 512, F32)[0:1, :]
    y1 = cv(AR5, 12288, 512, F32)
    t_m = cv(AR5, 14336, 512, F32)
    ea_g = cv(AR5, 16384, 512, BF16)
    g0rep = cv(AR5, 17408, 512, BF16)
    ghat_v = [cv(AR5, 18432, 512, BF16), cv(AR5, 19456, 512, BF16)]
    dmin_v = cv(AR5, 20480, 512, F32)
    hsd = cv(AR5, 0, 512, F32)
    sdec = cv(AR5, 2048, 512, F32)
    sjunk = cv(AR5, 6144, 512, F32)
    sinit_v = cv(AR5, 8192, 512, F32)
    S32 = cv(AR4, 0, INTER, F32)
    Sbf = cv(AR4, 16384, INTER, BF16)
    XW = cv(AR4, 24576, INTER, BF16)
    sg_v = cv(AR3, 0, INTER, BF16)
    drep = cv(AR3, 8192, INTER, BF16)
    e01 = AR3[0:64, 8192:12288]
    dmatL = AR2[:, 0:8192]
    dmatR = AR2[:, 8192:16384]
    r1fm = AR2[:, 0:NKC * T]
    xqT = AR3[:, 0:NKC * T]
    mixtt = cv(AR5, 0, HID, F32)
    xts = cv(AR5, 8192, HID, F32)
    zbuf = cv(AR4, 0, HID, F32)
    xqbf = cv(AR4, 8192, HID, BF16)
    r1bf = cv(AR4, 12288, HID, BF16)
    wopb = [cv(AR4, 16384 + i * 1024, 512, BF16) for i in range(2)]
    cjunk = cv(AR5, 16384, HID, F32)
    wgb = AR4[:, 0:8192]
    wub = AR4[:, 8192:16384]
    p_fm = XT[:]
    wdb = [cv(AR5, 0 + i * 1024, 512, BF16) for i in range(2)]
    sgev = cv(AR5, 2048, 512, BF16)
    partst = cv(AR5, 3072, T, F32)
    r2c = cv(AR5, 7168, T, F32)
    sqc = cv(AR5, 11264, T, F32)
    outc = cv(AR5, 15360, T, F32)
    meanbc = cv(AR5, 19456, T, F32)
    rstdbc = cv(AR5, 23552, T, F32)

    pb = [ec(nc.psum_tensor(f"pb{i}", [128, 512], F32)) for i in range(8)]
    PB = [f"pb{i}" for i in range(8)]

    p.dma("gpsimd", lambda g: g.dma_start(out=ident[:], in_=ident_d), ["d"], ["ident"])
    p.dma("sync", lambda g: g.dma_start(out=cumut[:], in_=cumut_d), ["d"], ["cumut"])
    p.dma("gpsimd", lambda g: g.dma_start(out=mask01[:], in_=cumut_d), ["d"], ["mask01"])
    p.dma("gpsimd", lambda g: g.dma_start(out=slo4[:], in_=slo4_d), ["d"], ["slo4"])
    p.dma("gpsimd", lambda g: g.dma_start(out=negid[:], in_=negid_d), ["d"], ["negid"])
    p.dma("sync", lambda g: g.dma_start(out=selm[:], in_=selm_d), ["d"], ["selm"])
    p.dma("sync", lambda g: g.dma_start(out=convb_sb[:], in_=convb_d), ["d"], ["convb"])
    p.dma("sync", lambda g: g.dma_start(out=nw_sb[:], in_=nw_d), ["d"], ["nw"])
    p.v(lambda v: v.memset(ones_col[:], 1.0), [], ["ones_col"])
    p.v(lambda v: v.memset(ones_r32[:], 1.0), [], ["ones_r32"])
    p.v(lambda v: v.memset(ones_rb[:], 1.0), [], ["ones_rb"])
    p.v(lambda v: v.memset(eps5[:], 1e-5), [], ["eps5"])
    p.v(lambda v: v.memset(eps8[:], 1e-8), [], ["eps8"])

    def load_x(src):
        p.dma("gpsimd",
              lambda g: g.dma_start(out=x_fm_v.rearrange("p (k t) -> p k t", k=NKC),
                                    in_=src.rearrange("(k p) t -> p k t", p=128)),
              ["d"], ["AR2"], tag="ldx")

    p.dma("sync", lambda g: g.dma_start(out=rbuf[:], in_=dtb_d), ["d"], ["rbuf"])
    p.t(lambda t: t.matmul(pb[0][:, :], ones_r32[:], rbuf[:], start=True, stop=True),
        ["ones_r32", "rbuf"], [PB[0]])
    p.v(lambda v: v.tensor_copy(dtb_bc[:], pb[0][:, :]), [PB[0]], ["dtb_bc"])
    p.dma("sync", lambda g: g.dma_start(out=rbuf[:], in_=alog_d), ["rbuf", "dtb_bc"], ["rbuf"])
    p.t(lambda t: t.matmul(pb[0][:, :], ones_r32[:], rbuf[:], start=True, stop=True),
        ["ones_r32", "rbuf"], [PB[0]])
    p.s(lambda s: s.activation(arep_bc[:], pb[0][:, :], AF.Exp), [PB[0]], ["arep_bc"])
    p.v(lambda v: v.tensor_scalar(arep_bc[:], arep_bc[:], -1.0, None, ALU.mult),
        ["arep_bc"], ["arep_bc"])
    drw = sb("drw", [1, 64], BF16)
    p.dma("gpsimd", lambda g: g.dma_start(out=drw[:], in_=drow_d), ["d"], ["drw"])
    p.t(lambda t: t.matmul(pb[1][0:64, 0:128], drw[:], ones_rb[:], start=True, stop=True),
        ["drw", "ones_rb"], [PB[1]])
    p.v(lambda v: v.tensor_copy(d_cr[:], pb[1][0:64, 0:128]), [PB[1]], ["d_cr"])

    def load_panel(pan):
        i = pan % 2
        keys = [f"wpan{i}"] + (["e01"] if i == 1 else [])
        p.dma("gpsimd",
              lambda g, pan=pan, i=i: g.dma_start(
                  out=wpan[i].rearrange("p (k j) -> p k j", k=NKC),
                  in_=wip_d[pan].rearrange("k p j -> p k j")),
              ["d"], keys, tag=f"ldpan{pan}")

    XTr = XT[:].rearrange("p (tt c) -> p tt c", tt=NTT)

    def phase_a(full):
      pans = list(range(NPAN)) if full else list(range(8, NPAN))
      load_panel(pans[0])
      load_panel(pans[1])
      p.dma("gpsimd", lambda g: g.dma_start(out=convd_v[0].rearrange("p (k j) -> p k j", k=4),
                                            in_=convd_d[0].rearrange("k p j -> p k j")),
            ["d"], ["convd0"], tag="ldcv")
      for pi, pan in enumerate(pans):
        i = pan % 2
        wk = f"wpan{i}"
        wv = wpan[i].rearrange("p (k j) -> p k j", k=NKC)
        if pan < 8:
            for tt in range(NTT):
                bank = tt % 2
                for kc in range(NKC):
                    p.t(lambda t, kc=kc, tt=tt, bank=bank, wv=wv:
                        t.matmul(pb[bank][:, :],
                                 x_fm_v[:, kc * TE + 3 + tt * 128: kc * TE + 3 + (tt + 1) * 128],
                                 wv[:, kc, :], start=(kc == 0), stop=(kc == NKC - 1)),
                        ["AR2", wk], [PB[bank]], tag=f"gmm{pan}_{tt}")
                p.s(lambda s, bank=bank: s.copy(gate_st[:, bank * 512:(bank + 1) * 512], pb[bank][:, :]),
                    [PB[bank]], [f"gst{bank}"], tag="gevac")
                p.dma("sync",
                      lambda g, pan=pan, tt=tt, bank=bank: g.dma_start(
                          out=gate_dram[pan, tt], in_=gate_st[:, bank * 512:(bank + 1) * 512]),
                      [f"gst{bank}"], [f"g{pan}_{tt}"], tag="gspill")
                if tapt("gate", [8, NTT, 128, 512], BF16) is not None:
                    p.dma("sync",
                          lambda g, pan=pan, tt=tt, bank=bank: g.dma_start(
                              out=tap_d["gate"][pan, tt], in_=gate_st[:, bank * 512:(bank + 1) * 512]),
                          [f"gst{bank}"], ["OUTD"])
        else:
            ocs = [(pan - 8) * 4 + j for j in range(4)]
            ocs = [oc for oc in ocs if oc < 34]
            for oc in ocs:
                slot = oc % 2
                cslot = oc % 2
                nxt = oc + 1
                if nxt < 34:
                    p.dma("gpsimd", lambda g, nxt=nxt: g.dma_start(
                        out=convd_v[nxt % 2].rearrange("p (k j) -> p k j", k=4),
                        in_=convd_d[nxt].rearrange("k p j -> p k j")),
                        ["d"], [f"convd{nxt % 2}"], tag="ldcv")
                for (j0, j1, bank) in ((0, 512, 0), (512, 1024, 1), (1024, TE, 2)):
                    for kc in range(NKC):
                        p.t(lambda t, kc=kc, oc=oc, j0=j0, j1=j1, bank=bank, wv=wv, pan=pan:
                            t.matmul(pb[bank][:, 0:j1 - j0],
                                     wv[:, kc, (oc - (pan - 8) * 4) * 128:(oc - (pan - 8) * 4 + 1) * 128],
                                     x_fm_v[:, kc * TE + j0: kc * TE + j1],
                                     start=(kc == 0), stop=(kc == NKC - 1)),
                            ["AR2", wk], [PB[bank]], tag=f"hmm{oc}")
                    p.s(lambda s, slot=slot, j0=j0, j1=j1, bank=bank:
                        s.copy(hbc_v[:, slot * TE + j0: slot * TE + j1], pb[bank][:, 0:j1 - j0]),
                        [PB[bank]], [f"hbc{slot}"], tag="hevac")
                for (j0, bank) in ((0, 3), (512, 4)):
                    for k in range(4):
                        p.t(lambda t, oc=oc, slot=slot, j0=j0, k=k, bank=bank, cslot=cslot:
                            t.matmul(pb[bank][:, :],
                                     convd_v[cslot][:, k * 128:(k + 1) * 128],
                                     hbc_v[:, slot * TE + j0 + k: slot * TE + j0 + k + 512],
                                     start=(k == 0), stop=(k == 3)),
                            [f"hbc{slot}", f"convd{cslot}"], [PB[bank]], tag=f"cvmm{oc}")
                    if oc < 32:
                        p.s(lambda s, oc=oc, j0=j0, bank=bank:
                            s.activation(hs_v[:, j0:j0 + 512], pb[bank][:, :], AF.Silu,
                                         bias=convb_sb[:, oc:oc + 1]),
                            [PB[bank], "convb"], ["hs"], tag="silu_hs")
                    elif oc == 32:
                        p.s(lambda s, j0=j0, bank=bank:
                            s.activation(B_fm[:, j0:j0 + 512], pb[bank][:, :], AF.Silu,
                                         bias=convb_sb[:, 32:33]),
                            [PB[bank], "convb"], ["B_fm"], tag="silu_B")
                    else:
                        p.s(lambda s, j0=j0, bank=bank:
                            s.activation(C_fm[:, j0:j0 + 512], pb[bank][:, :], AF.Silu,
                                         bias=convb_sb[:, 33:34]),
                            [PB[bank], "convb"], ["C_fm"], tag="silu_C")
                if oc < 32:
                    for half in range(2):
                        bank = 5 + half
                        for q in range(4):
                            tt = half * 4 + q
                            p.t(lambda t, tt=tt, q=q, bank=bank:
                                t.matmul(pb[bank][:, q * 128:(q + 1) * 128],
                                         hs_v[:, tt * 128:(tt + 1) * 128], ident[:],
                                         start=(q == 0), stop=(q == 3)),
                                ["hs", "ident"], [PB[bank]], tag=f"trp{oc}")
                        p.v(lambda v, half=half, oc=oc, bank=bank:
                            v.tensor_copy(
                                XTr[:, half * 4:(half + 1) * 4, oc * 128:(oc + 1) * 128],
                                pb[bank][:, :].rearrange("p (q c) -> p q c", q=4)),
                            [PB[bank]], ["XT"], tag="trpev")
            if pan == 16:
                for tt in range(NTT):
                    for kc in range(NKC):
                        p.t(lambda t, kc=kc, tt=tt, wv=wv:
                            t.matmul(pb[7][:, tt * 64:(tt + 1) * 64],
                                     x_fm_v[:, kc * TE + 3 + tt * 128: kc * TE + 3 + (tt + 1) * 128],
                                     wv[:, kc, 256:320],
                                     start=(kc == 0 and tt == 0), stop=(kc == NKC - 1 and tt == NTT - 1)),
                            ["AR2", wk], [PB[7]], tag="dtmm")
                p.v(lambda v: v.tensor_copy(dt_tok[:], pb[7][:, :]), [PB[7]], ["dt_tok"])
        if pi + 2 < len(pans):
            load_panel(pans[pi + 2])
      for tt in range(NTT):
        bank = 5 + (tt % 2)
        p.t(lambda t, tt=tt, bank=bank:
            t.matmul(pb[bank][:, 0:128], B_fm[:, tt * 128:(tt + 1) * 128], ident[:],
                     start=True, stop=True),
            ["B_fm", "ident"], [PB[bank]], tag="bT")
        p.v(lambda v, tt=tt, bank=bank:
            v.tensor_copy(BT[:, tt * 128:(tt + 1) * 128], pb[bank][:, 0:128]),
            [PB[bank]], ["BT"])
      if full:
        if tapt("XT", [128, NTT * INTER], BF16) is not None:
            p.dma("sync", lambda g: g.dma_start(out=tap_d["XT"], in_=XT[:]), ["XT"], ["OUTD"])
        if tapt("B", [128, T], BF16) is not None:
            p.dma("sync", lambda g: g.dma_start(out=tap_d["B"], in_=B_fm[:]), ["B_fm"], ["OUTD"])
        if tapt("C", [128, T], BF16) is not None:
            p.dma("sync", lambda g: g.dma_start(out=tap_d["C"], in_=C_fm[:]), ["C_fm"], ["OUTD"])
        if tapt("dt_raw", [128, 512]) is not None:
            p.dma("sync", lambda g: g.dma_start(out=tap_d["dt_raw"], in_=dt_tok[:]), ["dt_tok"], ["OUTD"])

    def scan_prep(full):
      p.v(lambda v: v.tensor_add(sp_dt[:], dt_tok[:], dtb_bc[:]), ["dt_tok", "dtb_bc"], ["sp_dt"])
      p.s(lambda s: s.activation(t_ab[:], sp_dt[:], AF.Abs), ["sp_dt"], ["t_ab"])
      p.s(lambda s: s.activation(t_ab[:], t_ab[:], AF.Exp, scale=-1.0), ["t_ab"], ["t_ab"])
      p.s(lambda s: s.activation(t_ab[:], t_ab[:], AF.Ln, bias=ones_col[:, 0:1]), ["t_ab", "ones_col"], ["t_ab"])
      p.v(lambda v: v.tensor_scalar_max(sp_dt[:], sp_dt[:], 0.0), ["sp_dt"], ["sp_dt"])
      p.v(lambda v: v.tensor_add(sp_dt[:], sp_dt[:], t_ab[:]), ["sp_dt", "t_ab"], ["sp_dt"])
      p.s(lambda s: s.activation(lndt[:], sp_dt[:], AF.Ln), ["sp_dt"], ["lndt"])
      if full and tapt("dt", [128, 512]) is not None:
        p.dma("sync", lambda g: g.dma_start(out=tap_d["dt"], in_=sp_dt[:]), ["sp_dt"], ["OUTD"])
      p.v(lambda v: v.tensor_mul(dta[:], sp_dt[:], arep_bc[:]), ["sp_dt", "arep_bc"], ["dta"])
      p.t(lambda t: t.matmul(pb[0][:, :], cumut[:], dta[:], start=True, stop=True),
          ["cumut", "dta"], [PB[0]], tag="cumsum")
      p.v(lambda v: v.tensor_copy(ahat[:], pb[0][:, :]), [PB[0]], ["ahat"])
      if full and tapt("ahat", [128, 512]) is not None:
        p.dma("sync", lambda g: g.dma_start(out=tap_d["ahat"], in_=ahat[:]), ["ahat"], ["OUTD"])
      p.dma("gpsimd", lambda g: g.dma_start(out=arow[:], in_=ahat[127:128, :]), ["ahat"], ["arow"])
      p.t(lambda t: t.matmul(pb[1][:, :], ones_r32[:], arow[:], start=True, stop=True),
          ["ones_r32", "arow"], [PB[1]], tag="arowb")
      p.v(lambda v: v.tensor_sub(wsp[:], pb[1][:, :], ahat[:]), [PB[1], "ahat"], ["wsp"])
      p.v(lambda v: v.tensor_add(wsp[:], wsp[:], lndt[:]), ["wsp", "lndt"], ["wsp"])
      p.s(lambda s: s.activation(wsb_bf[:], wsp[:], AF.Exp), ["wsp"], ["wsb_bf"])
      if full:
        p.v(lambda v: v.tensor_scalar(lndt[:], lndt[:], -1.0, None, ALU.mult),
            ["lndt", "wsp"], ["lndt"], tag="negln")
      p.s(lambda s: s.activation(lam_row[:], arow[:], AF.Exp), ["arow"], ["lam_row"])
      ea_tmp = cv(AR5, 8192, 512, BF16)
      if full:
        p.s(lambda s: s.activation(ea_tmp[:], ahat[:], AF.Exp), ["ahat", "dta"], ["ea_tmp"])
      for tt in range(NTT):
        p.t(lambda t, tt=tt: t.matmul(pb[2][0:64, 0:128], lam_row[:, tt * 64:(tt + 1) * 64],
                                      ones_rb[:], start=True, stop=True),
            ["lam_row", "ones_rb"], [PB[2]], tag="lamcr")
        p.v(lambda v, tt=tt: v.tensor_copy(lam_cr[:, tt * 128:(tt + 1) * 128], pb[2][0:64, 0:128]),
            [PB[2]], ["lam_cr"])
      srcs = [("wsb_bf", wsb_bf, w_fm, "w_fm")]
      if full:
        srcs.append(("ea_tmp", ea_tmp, ea_fm, "ea_fm"))
      for (srck, src, dst, dstk) in srcs:
        for tt in range(NTT):
            bank = 3 + tt % 2
            p.t(lambda t, tt=tt, src=src, bank=bank:
                t.matmul(pb[bank][0:64, 0:128], src[:, tt * 64:(tt + 1) * 64], ident[:],
                         start=True, stop=True),
                [srck, "ident"], [PB[bank]], tag="wfm")
            p.v(lambda v, tt=tt, dst=dst, bank=bank, dstk=dstk:
                v.tensor_copy(dst[:, tt * 128:(tt + 1) * 128], pb[bank][0:64, 0:128]),
                [PB[bank]], [dstk])
      if not full:
        return
      neg_a = cv(AR5, 10240, 512, F32)
      p.v(lambda v: v.tensor_scalar(neg_a[:], ahat[:], -1.0, None, ALU.mult), ["ahat", "wsp"], ["neg_a"])
      for (name, src, srck) in (("hi", ahat, "ahat"), ("hi_n", neg_a, "neg_a"), ("lnhi", lndt, "lndt")):
        lo = {"hi": "lo", "hi_n": "lo_n", "lnhi": "lnlo"}[name]
        p.v(lambda v, name=name, src=src: v.tensor_copy(hilo[name][:], src[:]), [srck], ["hl" + name])
        p.v(lambda v, name=name, lo=lo, src=src: v.tensor_sub(hilo[lo][:], src[:], hilo[name][:]),
            [srck, "hl" + name], ["hl" + lo])
      ARRS = ["hi", "lo", "hi_n", "lo_n", "lnhi", "lnlo"]
      for tt in range(NTT):
        for ai, name in enumerate(ARRS):
            bank = 5 + (ai % 2)
            p.t(lambda t, tt=tt, name=name, bank=bank:
                t.matmul(pb[bank][0:64, 0:128], hilo[name][:, tt * 64:(tt + 1) * 64], ident[:],
                         start=True, stop=True),
                ["hl" + name, "ident"], [PB[bank]], tag="flT")
            p.v(lambda v, bank=bank: v.tensor_copy(stage64[0:64, 0:128], pb[bank][0:64, 0:128]),
                [PB[bank]], ["st64"], tag="flev")
            p.dma("sync", lambda g, tt=tt, ai=ai: g.dma_start(out=flat_dram[tt, ai], in_=stage64[0:64, 0:128]),
                  ["st64"], [f"fl{tt}_{ai}"], tag="fldma")

    def load_e01():
        p.dma("gpsimd", lambda g: g.dma_start(out=e01[:], in_=e01_d),
              ["d"], ["e01", "wpan1"], tag="lde01")

    def drep_build():
        for g_ in range(8):
            p.t(lambda t, g2=g_: t.matmul(pb[4][:, :], d_cr[:], e01[:, g2 * 512:(g2 + 1) * 512],
                                          start=True, stop=True),
                ["d_cr", "e01"], [PB[4]], tag="drep")
            p.v(lambda v, g2=g_: v.tensor_copy(drep[:, g2 * 512:(g2 + 1) * 512], pb[4][:, :]),
                [PB[4]], ["drep"])
        p.v(lambda v: v.memset(dmatL[0:6, :], 1.0), ["dmatL"], ["dmatL"])
        p.v(lambda v: v.memset(dmatR[0:6, :], 1.0), ["dmatR"], ["dmatR"])

    def load_delta(tt):
        p.dma("sync", lambda g, tt=tt: g.dma_start(
            out=dmatL[0:2, :].rearrange("a (h t) -> a h t", h=64),
            in_=flat_dram[tt, 0:2]), [f"fl{tt}_0", f"fl{tt}_1", "dmatL"], ["dmatL"], tag="dlL")
        p.dma("sync", lambda g, tt=tt: g.dma_start(
            out=dmatL[4:6, :].rearrange("a (h t) -> a h t", h=64),
            in_=flat_dram[tt, 4:6]), [f"fl{tt}_4", f"fl{tt}_5", "dmatL"], ["dmatL"], tag="dlL")
        p.dma("sync", lambda g, tt=tt: g.dma_start(
            out=dmatR[2:4, :].rearrange("a (h t) -> a h t", h=64),
            in_=flat_dram[tt, 2:4]), [f"fl{tt}_2", f"fl{tt}_3", "dmatR"], ["dmatR"], tag="drL")


    def state_chunk(tt, phase):
        for g_ in range(8):
            p.t(lambda t, tt=tt, g2=g_: t.matmul(pb[0][:, :], w_fm[:, tt * 128:(tt + 1) * 128],
                                                 e01[:, g2 * 512:(g2 + 1) * 512], start=True, stop=True),
                ["w_fm", "e01"], [PB[0]], tag=f"{phase}wx")
            p.v(lambda v, tt=tt, g2=g_: v.tensor_mul(XW[:, g2 * 512:(g2 + 1) * 512], pb[0][:, :],
                                                     XT[:, tt * INTER + g2 * 512: tt * INTER + (g2 + 1) * 512]),
                [PB[0], "XT"], ["XW"], tag=f"{phase}xw")
            p.t(lambda t, tt=tt, g2=g_: t.matmul(pb[1][:, :], BT[:, tt * 128:(tt + 1) * 128],
                                                 XW[:, g2 * 512:(g2 + 1) * 512], start=True, stop=True),
                ["BT", "XW"], [PB[1]], tag=f"{phase}ds")
            p.t(lambda t, tt=tt, g2=g_: t.matmul(pb[2][:, :], lam_cr[:, tt * 128:(tt + 1) * 128],
                                                 e01[:, g2 * 512:(g2 + 1) * 512], start=True, stop=True),
                ["lam_cr", "e01"], [PB[2]], tag=f"{phase}lx")
            p.v(lambda v, g2=g_: v.tensor_mul(sdec[:], S32[:, g2 * 512:(g2 + 1) * 512], pb[2][:, :]),
                ["S32", PB[2]], ["sdec"], tag=f"{phase}dec")
            p.v(lambda v, g2=g_: v.tensor_add(S32[:, g2 * 512:(g2 + 1) * 512], sdec[:], pb[1][:, :]),
                ["sdec", PB[1]], ["S32"], tag=f"{phase}sup")

    def b1_pass():
        p.v(lambda v: v.memset(S32[:], 0.0), ["AR4"], ["S32"])
        for tt in range(NTT):
            state_chunk(tt, "b1")
        if tapt("S8", [N, INTER]) is not None:
            p.dma("sync", lambda g: g.dma_start(out=tap_d["S8"], in_=S32[:]), ["S32"], ["OUTD"])
        p.v(lambda v: v.tensor_scalar(S32[:], S32[:], selm[:, 0:1], None, ALU.mult),
            ["S32", "selm"], ["S32"], tag="sinm")
        p.dma("sync", lambda g: g.dma_start(out=sinit_dram, in_=S32[:]), ["S32"], ["ccin"],
              tag="ccin")

    load_x(x_prev_d)
    phase_a(False)
    scan_prep(False)
    p.fence()
    load_e01()
    b1_pass()
    p.fence()

    load_x(x_fm_d)
    phase_a(True)
    scan_prep(True)
    p.fence()
    load_e01()
    drep_build()
    load_delta(0)
    p.dma("sync", lambda g: g.dma_start(out=S32[:], in_=sinit_dram),
          ["ccin", "S32"], ["S32"], tag="sinld")

    Sbf2 = cv(AR5, 20480, INTER, BF16)
    Sbf_v = [Sbf, Sbf2]
    y1_v = [y1, cv(AR4, 28672, 512, F32)]
    t_m_v = [t_m, cv(AR4, 30720, 512, F32)]
    ea_g_v = [ea_g, cv(AR5, 4096, 512, BF16)]
    hsd_v = [hsd, cv(AR5, 10240, 512, F32)]

    def b2_intra(tt):
        par = tt % 2
        sbc, sbk = Sbf_v[par], f"Sbf{par}"
        for pan in range(8):
            p.dma("sync", lambda g, tt=tt, pan=pan: g.dma_start(
                out=sg_v[:, pan * 512:(pan + 1) * 512], in_=gate_dram[pan, tt]),
                [f"g{pan}_{tt}", f"SG{pan}"], [f"SG{pan}"], tag="gld")
        p.s(lambda s: s.activation(sg_v[:], sg_v[:], AF.Silu),
            [f"SG{i}" for i in range(8)], [f"SG{i}" for i in range(8)], tag="sgsilu")
        p.t(lambda t, tt=tt: t.matmul(pb[3][:, 0:128], B_fm[:, tt * 128:(tt + 1) * 128],
                                      C_fm[:, tt * 128:(tt + 1) * 128], start=True, stop=True),
            ["B_fm", "C_fm"], [PB[3]], tag="g0")
        p.v(lambda v: v.tensor_mul(g0rep[:, 0:128], pb[3][:, 0:128], mask01[:]),
            [PB[3], "mask01"], ["G0R"], tag="g0m")
        for q in range(1, 4):
            p.s(lambda s, q=q: s.copy(g0rep[:, q * 128:(q + 1) * 128], g0rep[:, 0:128]),
                ["G0R"], ["G0R"], tag="g0rep")
        for hg in range(16):
            gb = hg % 2
            for j in range(4):
                h = hg * 4 + j
                p.t(lambda t, h=h, j=j, gb=gb:
                    t.matmul(pb[4 + gb][:, j * 128:(j + 1) * 128],
                             dmatL[0:6, h * 128:(h + 1) * 128],
                             dmatR[0:6, h * 128:(h + 1) * 128],
                             start=(j == 0), stop=False),
                    ["dmatL", "dmatR"], [PB[4 + gb]], tag="dmm")
            p.t(lambda t, gb=gb:
                t.matmul(pb[4 + gb][:, :], negid[:], slo4[:], start=False, stop=True),
                ["negid", "slo4"], [PB[4 + gb]], tag="dmmtri")
            p.s(lambda s, gb=gb: s.activation(ghat_v[gb][:], pb[4 + gb][:, :], AF.Exp,
                                              scale=-1.0),
                [PB[4 + gb]], [f"GH{gb}"], tag="eexp")
            p.v(lambda v, gb=gb: v.tensor_mul(ghat_v[gb][:], ghat_v[gb][:], g0rep[:]),
                [f"GH{gb}", "G0R"], [f"GH{gb}"], tag="ghat")
            for j in range(4):
                h = hg * 4 + j
                p.t(lambda t, tt=tt, h=h, j=j, gb=gb:
                    t.matmul(pb[6][:, (h % 8) * 64:((h % 8) + 1) * 64],
                             ghat_v[gb][:, j * 128:(j + 1) * 128],
                             XT[:, tt * INTER + h * 64: tt * INTER + (h + 1) * 64],
                             start=(h % 8 == 0 and j == 0), stop=(h % 8 == 7 and j == 3)),
                    [f"GH{gb}", "XT"], [PB[6]], tag="yi")
            if hg % 2 == 1:
                g_ = hg // 2
                pr = g_ % 2
                y1c, y1k = y1_v[pr], f"y1{pr}"
                tmc, tmk = t_m_v[pr], f"t_m{pr}"
                eac, eak = ea_g_v[pr], f"ea_g{pr}"
                hsc, hsk = hsd_v[pr], f"HSD{pr}"
                p.t(lambda t, tt=tt, g2=g_, sbc=sbc: t.matmul(pb[7][:, :], C_fm[:, tt * 128:(tt + 1) * 128],
                                                     sbc[:, g2 * 512:(g2 + 1) * 512], start=True, stop=True),
                    ["C_fm", sbk], [PB[7]], tag="yin")
                p.t(lambda t, tt=tt, g2=g_: t.matmul(pb[3][:, :], ea_fm[:, tt * 128:(tt + 1) * 128],
                                                     e01[:, g2 * 512:(g2 + 1) * 512], start=True, stop=True),
                    ["ea_fm", "e01"], [PB[3]], tag="eax")
                p.s(lambda s, eac=eac: s.copy(eac[:], pb[3][:, :]), [PB[3]], [eak], tag="eaev")
                p.v(lambda v, tmc=tmc, eac=eac: v.tensor_mul(tmc[:], pb[7][:, :], eac[:]),
                    [PB[7], eak], [tmk], tag="minter")
                p.v(lambda v, y1c=y1c, tmc=tmc: v.tensor_add(y1c[:], tmc[:], pb[6][:, :]),
                    [tmk, PB[6]], [y1k], tag="yadd")
                p.pool(lambda g, tt=tt, g2=g_, hsc=hsc: g.tensor_mul(hsc[:], XT[:, tt * INTER + g2 * 512: tt * INTER + (g2 + 1) * 512],
                                                            drep[:, g2 * 512:(g2 + 1) * 512]),
                       ["XT", "drep", hsk], [hsk], tag="dhs")
                p.pool(lambda g, y1c=y1c, hsc=hsc: g.tensor_add(y1c[:], y1c[:], hsc[:]),
                       [y1k, hsk], [y1k], tag="yadd2")
                p.v(lambda v, g2=g_, y1c=y1c: v.tensor_mul(sg_v[:, g2 * 512:(g2 + 1) * 512], y1c[:],
                                                  sg_v[:, g2 * 512:(g2 + 1) * 512]),
                    [y1k, f"SG{g_}"], [f"SG{g_}"], tag="ygate")
                p.s(lambda s, g2=g_: s.activation(sjunk[:], sg_v[:, g2 * 512:(g2 + 1) * 512], AF.Square,
                                                  accum_out=sums8[:, g2:g2 + 1]),
                    [f"SG{g_}", "SJ"], ["SJ", "sums8"], tag="sumsq")
        p.v(lambda v: v.tensor_reduce(stat[:, 0:1], sums8[:], AX.X, ALU.add),
            ["sums8", "stat"], ["stat"], tag="rsum")
        p.v(lambda v: v.tensor_scalar(stat[:, 1:2], stat[:, 0:1], INV_INTER, None, ALU.mult),
            ["stat"], ["stat"], tag="rvar")
        p.s(lambda s: s.activation(stat[:, 2:3], stat[:, 1:2], AF.Ln, bias=eps5[:, 0:1]),
            ["stat", "eps5"], ["stat"], tag="rln")
        p.s(lambda s: s.activation(stat[:, 3:4], stat[:, 2:3], AF.Exp, scale=-0.5),
            ["stat"], ["stat"], tag="rexp")
        for g_ in range(8):
            p.v(lambda v, g2=g_: v.tensor_scalar(sg_v[:, g2 * 512:(g2 + 1) * 512],
                                                 sg_v[:, g2 * 512:(g2 + 1) * 512], stat[:, 3:4],
                                                 None, ALU.mult),
                [f"SG{g_}", "stat"], [f"SG{g_}"], tag="ynorm")
        if tapt("yn", [NTT, 128, INTER], BF16) is not None:
            p.dma("sync", lambda g, tt=tt: g.dma_start(out=tap_d["yn"][tt], in_=sg_v[:]),
                  [f"SG{i}" for i in range(8)], ["OUTD"])
        for half in range(2):
            for qq in range(4):
                bank = 4 + half
                for q in range(4):
                    oc2 = half * 16 + qq * 4 + q
                    p.t(lambda t, oc2=oc2, q=q, bank=bank:
                        t.matmul(pb[bank][:, q * 128:(q + 1) * 128],
                                 sg_v[:, oc2 * 128:(oc2 + 1) * 128], ident[:],
                                 start=(q == 0), stop=(q == 3)),
                        [f"SG{oc2 // 4}", "ident"], [PB[bank]], tag="ynT")
                for q in range(4):
                    oc2 = half * 16 + qq * 4 + q
                    p.v(lambda v, tt=tt, oc2=oc2, q=q, bank=bank:
                        v.tensor_scalar(XT[:, tt * INTER + oc2 * 128: tt * INTER + (oc2 + 1) * 128],
                                        pb[bank][:, q * 128:(q + 1) * 128],
                                        nw_sb[:, oc2:oc2 + 1], None, ALU.mult),
                        [PB[bank], "nw", "XT"], ["XT"], tag="ynTev")
        if tt + 1 < NTT:
            load_delta(tt + 1)

    p.s(lambda s: s.copy(Sbf_v[0][:], S32[:]), ["S32"], ["Sbf0"], tag="sbf")
    state_chunk(0, "b2")
    for tt in range(NTT):
        if tt + 1 < NTT:
            par = (tt + 1) % 2
            p.s(lambda s, par=par: s.copy(Sbf_v[par][:], S32[:]), ["S32"], [f"Sbf{par}"], tag="sbf")
            state_chunk(tt + 1, "b2")
        b2_intra(tt)

    p.fence()

    def load_wop(jg, kc):
        i = kc % 2
        p.dma("gpsimd", lambda g, jg=jg, kc=kc, i=i: g.dma_start(out=wopb[i][:], in_=wop_d[jg, kc]),
              ["d", f"wopb{i}"], [f"wopb{i}"], tag="wopld")

    for jg in range(4):
        if jg == 0:
            load_wop(0, 0)
            load_wop(0, 1)
        for kc in range(32):
            i = kc % 2
            for tt in range(NTT):
                p.t(lambda t, jg=jg, kc=kc, tt=tt, i=i:
                    t.matmul(pb[tt][:, :], XT[:, tt * INTER + kc * 128: tt * INTER + (kc + 1) * 128],
                             wopb[i][:], start=(kc == 0), stop=(kc == 31)),
                    [f"wopb{i}", "XT"], [PB[tt]], tag="opmm")
            if kc + 2 < 32:
                load_wop(jg, kc + 2)
            elif jg + 1 < 4:
                load_wop(jg + 1, kc - 30)
        for tt in range(NTT):
            p.v(lambda v, jg=jg, tt=tt: v.tensor_copy(mixtt[:, 0:512], pb[tt][:, :]),
                [PB[tt], "MIXTT"], ["MIXTT"], tag="opev")
            p.dma("sync", lambda g, jg=jg, tt=tt: g.dma_start(out=mix_dram[tt][:, jg * 512:(jg + 1) * 512],
                                                              in_=mixtt[:, 0:512]),
                  ["MIXTT"], [f"mix{tt}_{jg}"], tag="opspill")

    def layer_norm(buf, bufk, dst, dstk, epsc, epsk, junk, junkk, sc):
        p.v(lambda v: v.tensor_reduce(stat[:, sc:sc + 1], buf[:], AX.X, ALU.add),
            [bufk, "stat"], ["stat"], tag="lred")
        p.s(lambda s: s.activation(junk[:], buf[:], AF.Square, accum_out=stat[:, sc + 1:sc + 2]),
            [bufk, junkk, "stat"], [junkk, "stat"], tag="lsq")
        p.v(lambda v: v.tensor_scalar(stat[:, sc:sc + 1], stat[:, sc:sc + 1], INV_DM, None, ALU.mult),
            ["stat"], ["stat"], tag="lm")
        p.v(lambda v: v.tensor_scalar(stat[:, sc + 1:sc + 2], stat[:, sc + 1:sc + 2], INV_DM, None, ALU.mult),
            ["stat"], ["stat"], tag="lm2")
        p.v(lambda v: v.tensor_mul(stat[:, sc + 2:sc + 3], stat[:, sc:sc + 1], stat[:, sc:sc + 1]),
            ["stat"], ["stat"], tag="lmm")
        p.v(lambda v: v.tensor_sub(stat[:, sc + 1:sc + 2], stat[:, sc + 1:sc + 2], stat[:, sc + 2:sc + 3]),
            ["stat"], ["stat"], tag="lvar")
        p.s(lambda s: s.activation(stat[:, sc + 2:sc + 3], stat[:, sc + 1:sc + 2], AF.Ln, bias=epsc[:, 0:1]),
            ["stat", epsk], ["stat"], tag="lln")
        p.s(lambda s: s.activation(stat[:, sc + 2:sc + 3], stat[:, sc + 2:sc + 3], AF.Exp, scale=-0.5),
            ["stat"], ["stat"], tag="lrexp")
        p.v(lambda v: v.scalar_tensor_tensor(stat[:, sc + 3:sc + 4], stat[:, sc:sc + 1], -1.0,
                                             stat[:, sc + 2:sc + 3], ALU.mult, ALU.mult),
            ["stat"], ["stat"], tag="lnb")
        p.v(lambda v: v.tensor_scalar(dst[:], buf[:], stat[:, sc + 2:sc + 3],
                                      stat[:, sc + 3:sc + 4], ALU.mult, ALU.add),
            [bufk, "stat", dstk], [dstk], tag="lapp")

    r1fmr = r1fm.rearrange("p (k t2) -> p k t2", k=NKC)
    xqTr = xqT.rearrange("p (k t2) -> p k t2", k=NKC)
    for tt in range(NTT):
        p.dma("sync", lambda g, tt=tt: g.dma_start(out=xts[:], in_=x_tok_d[tt * 128:(tt + 1) * 128, :]),
              ["d", "XTS"], ["XTS"], tag="xtld")
        p.dma("sync", lambda g, tt=tt: g.dma_start(out=mixtt[:], in_=mix_dram[tt]),
              [f"mix{tt}_{jg2}" for jg2 in range(4)] + ["MIXTT"], ["MIXTT"], tag="mixld")
        p.v(lambda v: v.tensor_add(mixtt[:], mixtt[:], xts[:]), ["MIXTT", "XTS"], ["MIXTT"], tag="resid1")
        layer_norm(mixtt, "MIXTT", mixtt, "MIXTT", eps5, "eps5", cjunk, "CJ", 4)
        if tapt("r1", [NTT, 128, HID]) is not None:
            p.dma("sync", lambda g, tt=tt: g.dma_start(out=tap_d["r1"][tt], in_=mixtt[:]),
                  ["MIXTT"], ["OUTD"])
        p.v(lambda v: v.tensor_copy(r1bf[:], mixtt[:]), ["MIXTT", "R1BF"], ["R1BF"], tag="r1bf")
        for quad in range(4):
            bank = quad % 2
            for q in range(4):
                kc = quad * 4 + q
                p.t(lambda t, kc=kc, q=q, bank=bank:
                    t.matmul(pb[bank][:, q * 128:(q + 1) * 128], r1bf[:, kc * 128:(kc + 1) * 128], ident[:],
                             start=(q == 0), stop=(q == 3)),
                    ["R1BF", "ident"], [PB[bank]], tag="r1T")
            p.v(lambda v, tt=tt, quad=quad, bank=bank:
                v.tensor_copy(r1fmr[:, quad * 4:(quad + 1) * 4, tt * 128:(tt + 1) * 128],
                              pb[bank][:, :].rearrange("p (q c) -> p q c", q=4)),
                [PB[bank], "R1FM"], ["R1FM"], tag="r1Tev")
        layer_norm(mixtt, "MIXTT", zbuf, "ZB", eps8, "eps8", cjunk, "CJ", 8)
        p.v(lambda v: v.reduce_max(stat[:, 12:13], zbuf[:], AX.X, apply_absolute_value=True),
            ["ZB", "stat"], ["stat"], tag="amax")
        p.v(lambda v: v.reciprocal(stat[:, 13:14], stat[:, 12:13]), ["stat"], ["stat"], tag="ramax")
        p.v(lambda v: v.tensor_scalar(stat[:, 13:14], stat[:, 13:14], 127.0, None, ALU.mult),
            ["stat"], ["stat"], tag="sfac")
        p.v(lambda v: v.tensor_scalar(stat[:, 14:15], stat[:, 12:13], 1.0 / 127.0, None, ALU.mult),
            ["stat"], ["stat"], tag="rsfac")
        p.v(lambda v: v.tensor_scalar(zbuf[:], zbuf[:], stat[:, 13:14], C_MAGIC, ALU.mult, ALU.add),
            ["ZB", "stat"], ["ZB"], tag="q1")
        p.v(lambda v: v.tensor_scalar(zbuf[:], zbuf[:], C_MAGIC, None, ALU.subtract),
            ["ZB"], ["ZB"], tag="q2")
        p.v(lambda v: v.tensor_scalar(xqbf[:], zbuf[:], stat[:, 14:15], None, ALU.mult),
            ["ZB", "stat", "XQB"], ["XQB"], tag="q3")
        if tapt("xq", [NTT, 128, HID], BF16) is not None:
            p.dma("sync", lambda g, tt=tt: g.dma_start(out=tap_d["xq"][tt], in_=xqbf[:]), ["XQB"], ["OUTD"])
        for quad in range(4):
            bank = 2 + quad % 2
            for q in range(4):
                kc = quad * 4 + q
                p.t(lambda t, kc=kc, q=q, bank=bank:
                    t.matmul(pb[bank][:, q * 128:(q + 1) * 128], xqbf[:, kc * 128:(kc + 1) * 128], ident[:],
                             start=(q == 0), stop=(q == 3)),
                    ["XQB", "ident"], [PB[bank]], tag="xqTm")
            p.v(lambda v, tt=tt, quad=quad, bank=bank:
                v.tensor_copy(xqTr[:, quad * 4:(quad + 1) * 4, tt * 128:(tt + 1) * 128],
                              pb[bank][:, :].rearrange("p (q c) -> p q c", q=4)),
                [PB[bank], "XQT"], ["XQT"], tag="xqTev")

    p.fence()

    wgbr = wgb.rearrange("p (k j) -> p k j", k=NKC)
    wubr = wub.rearrange("p (k j) -> p k j", k=NKC)

    def load_gu(pan, half):
        for which, d_, buf, bufr in (("WGH", wg_d, wgb, None), ("WUH", wu_d, wub, None)):
            dst = (wgb if which == "WGH" else wub)[:, half * 4096:(half + 1) * 4096]
            p.dma("gpsimd", lambda g, pan=pan, half=half, d_=d_, dst=dst: g.dma_start(
                out=dst.rearrange("p (k j) -> p k j", k=8),
                in_=d_[pan, half * 8:(half + 1) * 8].rearrange("k p j -> p k j")),
                ["d", f"{which}{half}"], [f"{which}{half}"], tag="ldgu")

    for Hhalf in range(2):
        pans = list(range(Hhalf * 8, (Hhalf + 1) * 8))
        load_gu(pans[0], 0)
        for pi, pan in enumerate(pans):
            load_gu(pan, 1)
            p.s(lambda s: s.sign(wgb[:, 0:4096], wgb[:, 0:4096]), ["WGH0"], ["WGH0"], tag="sgn")
            p.s(lambda s: s.sign(wub[:, 0:4096], wub[:, 0:4096]), ["WUH0"], ["WUH0"], tag="sgn")
            done_h1_sign = [False]

            def sign_h1():
                if not done_h1_sign[0]:
                    p.s(lambda s: s.sign(wgb[:, 4096:8192], wgb[:, 4096:8192]), ["WGH1"], ["WGH1"], tag="sgn")
                    p.s(lambda s: s.sign(wub[:, 4096:8192], wub[:, 4096:8192]), ["WUH1"], ["WUH1"], tag="sgn")
                    done_h1_sign[0] = True

            for oc in range(4):
                ho = (pi * 4 + oc)
                for jg in range(2):
                    for kc in range(NKC):
                        if kc == 8:
                            sign_h1()
                        kh = kc // 8
                        p.t(lambda t, oc=oc, kc=kc, jg=jg:
                            t.matmul(pb[jg][:, :], wgbr[:, kc, oc * 128:(oc + 1) * 128],
                                     xqTr[:, kc, jg * 512:(jg + 1) * 512],
                                     start=(kc == 0), stop=(kc == NKC - 1)),
                            [f"WGH{kh}", "XQT"], [PB[jg]], tag="gmmD")
                        p.t(lambda t, oc=oc, kc=kc, jg=jg:
                            t.matmul(pb[2 + jg][:, :], wubr[:, kc, oc * 128:(oc + 1) * 128],
                                     xqTr[:, kc, jg * 512:(jg + 1) * 512],
                                     start=(kc == 0), stop=(kc == NKC - 1)),
                            [f"WUH{kh}", "XQT"], [PB[2 + jg]], tag="ummD")
                    p.s(lambda s, jg=jg: s.activation(sgev[:], pb[jg][:, :], AF.Silu),
                        [PB[jg], "SGEV"], ["SGEV"], tag="gsilu")
                    p.v(lambda v, ho=ho, jg=jg: v.tensor_mul(p_fm[:, ho * T + jg * 512: ho * T + (jg + 1) * 512],
                                                             pb[2 + jg][:, :], sgev[:]),
                        [PB[2 + jg], "SGEV", "XT"], ["XT"], tag="pmul")
            if pi + 1 < len(pans):
                load_gu(pans[pi + 1], 0)
        if tapt("p_fm", [2, 128, 32 * T], BF16) is not None:
            p.dma("sync", lambda g, Hhalf=Hhalf: g.dma_start(out=tap_d["p_fm"][Hhalf], in_=XT[:]), ["XT"], ["OUTD"])

        p.dma("gpsimd", lambda g, Hhalf=Hhalf: g.dma_start(out=wdb[0][:], in_=wd_d[0, Hhalf * 32]),
              ["d", "wdb0"], ["wdb0"], tag="ldwd")
        p.dma("gpsimd", lambda g, Hhalf=Hhalf: g.dma_start(out=wdb[1][:], in_=wd_d[0, Hhalf * 32 + 1]),
              ["d", "wdb1"], ["wdb1"], tag="ldwd")
        for opan in range(4):
            for kci in range(32):
                i = kci % 2
                p.s(lambda s, i=i: s.sign(wdb[i][:], wdb[i][:]), [f"wdb{i}"], [f"wdb{i}"], tag="sgnD")
                for oc in range(4):
                    for jg in range(2):
                        p.t(lambda t, i=i, kci=kci, oc=oc, jg=jg:
                            t.matmul(pb[oc * 2 + jg][:, :], wdb[i][:, oc * 128:(oc + 1) * 128],
                                     p_fm[:, kci * T + jg * 512: kci * T + (jg + 1) * 512],
                                     start=(kci == 0), stop=(kci == 31)),
                            [f"wdb{i}", "XT"], [PB[oc * 2 + jg]], tag="dmmD")
                nxt_k = kci + 2
                if nxt_k < 32:
                    p.dma("gpsimd", lambda g, opan=opan, Hhalf=Hhalf, nxt_k=nxt_k, i=i: g.dma_start(
                        out=wdb[i][:], in_=wd_d[opan, Hhalf * 32 + nxt_k]),
                        [f"wdb{i}"], [f"wdb{i}"], tag="ldwd")
                elif opan + 1 < 4:
                    p.dma("gpsimd", lambda g, opan=opan, Hhalf=Hhalf, nxt_k=nxt_k, i=i: g.dma_start(
                        out=wdb[i][:], in_=wd_d[opan + 1, Hhalf * 32 + nxt_k - 32]),
                        [f"wdb{i}"], [f"wdb{i}"], tag="ldwd")
            for oc in range(4):
                occ = opan * 4 + oc
                if Hhalf == 0:
                    p.v(lambda v, oc=oc: v.tensor_copy(partst[:, 0:512], pb[oc * 2][:, :]),
                        [PB[oc * 2], "PARTST"], ["PARTST"], tag="pev")
                    p.v(lambda v, oc=oc: v.tensor_copy(partst[:, 512:1024], pb[oc * 2 + 1][:, :]),
                        [PB[oc * 2 + 1], "PARTST"], ["PARTST"], tag="pev")
                    p.dma("sync", lambda g, occ=occ: g.dma_start(out=part_dram[occ], in_=partst[:]),
                          ["PARTST"], [f"part{occ}"], tag="pspill")
                else:
                    p.dma("sync", lambda g, occ=occ: g.dma_start(out=partst[:], in_=part_dram[occ]),
                          [f"part{occ}", "PARTST"], ["PARTST"], tag="pld")
                    p.v(lambda v, oc=oc: v.tensor_add(r2c[:, 0:512], pb[oc * 2][:, :], partst[:, 0:512]),
                        [PB[oc * 2], "PARTST", "R2C"], ["R2C"], tag="r2a")
                    p.v(lambda v, oc=oc: v.tensor_add(r2c[:, 512:1024], pb[oc * 2 + 1][:, :], partst[:, 512:1024]),
                        [PB[oc * 2 + 1], "PARTST", "R2C"], ["R2C"], tag="r2b")
                    p.v(lambda v, occ=occ: v.tensor_add(r2c[:], r2c[:], r1fm[:, occ * T:(occ + 1) * T]),
                        ["R2C", "R1FM"], ["R2C"], tag="r2c")
                    p.dma("sync", lambda g, occ=occ: g.dma_start(out=part_dram[occ], in_=r2c[:]),
                          ["R2C", f"part{occ}"], [f"part{occ}"], tag="r2spill")

    if tapt("r2", [16, 128, T]) is not None:
        p.dma("sync", lambda g: g.dma_start(out=tap_d["r2"], in_=part_dram),
              [f"part{i}" for i in range(16)], ["OUTD"])
    if tapt("r1fm", [128, NKC * T], BF16) is not None:
        p.dma("sync", lambda g: g.dma_start(out=tap_d["r1fm"], in_=r1fm), ["R1FM"], ["OUTD"])
    for occ in range(16):
        p.dma("sync", lambda g, occ=occ: g.dma_start(out=r2c[:], in_=part_dram[occ]),
              [f"part{occ}", "R2C"], ["R2C"], tag="er2ld")
        p.s(lambda s: s.square(sqc[:], r2c[:]), ["R2C", "SQC"], ["SQC"], tag="esq")
        for jg in range(2):
            p.t(lambda t, jg=jg, occ=occ: t.matmul(pb[4 + jg][0:1, :], ones_col[:, 0:1],
                                                   r2c[:, jg * 512:(jg + 1) * 512],
                                                   start=(occ == 0), stop=(occ == 15)),
                ["R2C", "ones_col"], [PB[4 + jg]], tag="stsum")
            p.t(lambda t, jg=jg, occ=occ: t.matmul(pb[6 + jg][0:1, :], ones_col[:, 0:1],
                                                   sqc[:, jg * 512:(jg + 1) * 512],
                                                   start=(occ == 0), stop=(occ == 15)),
                ["SQC", "ones_col"], [PB[6 + jg]], tag="stsq")
    rowst = cv(AR5, 0, T, F32)[0:1, :]
    rowst2 = cv(AR5, 11264, T, F32)[0:1, :]
    p.v(lambda v: v.tensor_copy(rowst[0:1, 0:512], pb[4][0:1, :]), [PB[4], "rowst"], ["rowst"], tag="rs")
    p.v(lambda v: v.tensor_copy(rowst[0:1, 512:1024], pb[5][0:1, :]), [PB[5], "rowst"], ["rowst"], tag="rs")
    p.v(lambda v: v.tensor_copy(rowst2[0:1, 0:512], pb[6][0:1, :]), [PB[6], "rowst2"], ["rowst2"], tag="rs")
    p.v(lambda v: v.tensor_copy(rowst2[0:1, 512:1024], pb[7][0:1, :]), [PB[7], "rowst2"], ["rowst2"], tag="rs")
    p.v(lambda v: v.tensor_scalar(rowst[:], rowst[:], INV_DM, None, ALU.mult), ["rowst"], ["rowst"], tag="meanr")
    p.v(lambda v: v.tensor_scalar(rowst2[:], rowst2[:], INV_DM, None, ALU.mult), ["rowst2"], ["rowst2"], tag="m2r")
    for jg in range(2):
        p.v(lambda v, jg=jg: v.tensor_mul(pb[2][0:1, :], rowst[0:1, jg * 512:(jg + 1) * 512],
                                          rowst[0:1, jg * 512:(jg + 1) * 512]),
            ["rowst", PB[2]], [PB[2]], tag="mmr")
        p.v(lambda v, jg=jg: v.tensor_sub(rowst2[0:1, jg * 512:(jg + 1) * 512],
                                          rowst2[0:1, jg * 512:(jg + 1) * 512], pb[2][0:1, :]),
            ["rowst2", PB[2]], ["rowst2"], tag="varr")
    p.s(lambda s: s.activation(rowst2[:], rowst2[:], AF.Ln, bias=eps5[0:1, 0:1]),
        ["rowst2", "eps5"], ["rowst2"], tag="lnr")
    p.s(lambda s: s.activation(rowst2[:], rowst2[:], AF.Exp, scale=-0.5), ["rowst2"], ["rowst2"], tag="rstdr")
    for jg in range(2):
        p.t(lambda t, jg=jg: t.matmul(pb[0][:, :], ones_r32[:], rowst[0:1, jg * 512:(jg + 1) * 512],
                                      start=True, stop=True), ["rowst", "ones_r32"], [PB[0]], tag="bm")
        p.v(lambda v, jg=jg: v.tensor_copy(meanbc[:, jg * 512:(jg + 1) * 512], pb[0][:, :]),
            [PB[0], "MEANBC"], ["MEANBC"], tag="bmev")
        p.t(lambda t, jg=jg: t.matmul(pb[1][:, :], ones_r32[:], rowst2[0:1, jg * 512:(jg + 1) * 512],
                                      start=True, stop=True), ["rowst2", "ones_r32"], [PB[1]], tag="br")
        p.v(lambda v, jg=jg: v.tensor_copy(rstdbc[:, jg * 512:(jg + 1) * 512], pb[1][:, :]),
            [PB[1], "RSTDBC"], ["RSTDBC"], tag="brev")
    for occ in range(16):
        p.dma("sync", lambda g, occ=occ: g.dma_start(out=r2c[:], in_=part_dram[occ]),
              [f"part{occ}", "R2C"], ["R2C"], tag="r2ld")
        p.v(lambda v: v.tensor_sub(outc[:], r2c[:], meanbc[:]), ["R2C", "MEANBC", "OUTC"], ["OUTC"], tag="osub")
        p.v(lambda v: v.tensor_mul(outc[:], outc[:], rstdbc[:]), ["OUTC", "RSTDBC"], ["OUTC"], tag="omul")
        p.dma("sync", lambda g, occ=occ: g.dma_start(out=out_d[occ * 128:(occ + 1) * 128, :], in_=outc[:]),
              ["OUTC"], [f"OUT{occ}"], tag="ost")

    p.pool(lambda g: g.tensor_copy(stat[0:1, 0:1], stat[0:1, 0:1]),
           [f"OUT{i}" for i in range(16)] + ["OUTD", "stat"], ["fin"], tag="fin")

    p.emit(nc)
    return nc, sorted(tap_d.keys())



def _prep_shared(inputs):
    f32 = np.float32
    Wip = np.asarray(inputs["L0_attn_in_proj_w"], dtype=f32)
    WT = np.zeros((HID, NPAN * 512), dtype=f32)
    WT[:, :PROJ] = Wip.T
    wip = np.ascontiguousarray(WT.reshape(NKC, 128, NPAN, 512).transpose(2, 0, 1, 3))
    cw = np.asarray(inputs["L0_attn_conv_w"], dtype=f32)[:, 0, :]
    convd = np.zeros((34, 4, 128, 128), dtype=f32)
    idx = np.arange(128)
    cwr = cw.reshape(34, 128, 4)
    convd[:, :, idx, idx] = cwr.transpose(0, 2, 1)
    convb = np.ascontiguousarray(np.asarray(inputs["L0_attn_conv_b"], dtype=f32).reshape(34, 128).T)
    Wop = np.asarray(inputs["L0_attn_out_proj_w"], dtype=f32)
    wop = np.ascontiguousarray(Wop.T.reshape(32, 128, 4, 512).transpose(2, 0, 1, 3))
    Wg = np.asarray(inputs["L0_ffn_W_g"], dtype=f32)
    Wu = np.asarray(inputs["L0_ffn_W_u"], dtype=f32)
    Wd = np.asarray(inputs["L0_ffn_W_d"], dtype=f32)
    wg = np.ascontiguousarray(Wg.T.reshape(NKC, 128, 16, 512).transpose(2, 0, 1, 3))
    wu = np.ascontiguousarray(Wu.T.reshape(NKC, 128, 16, 512).transpose(2, 0, 1, 3))
    wd = np.ascontiguousarray(Wd.T.reshape(64, 128, 4, 512).transpose(2, 0, 1, 3))
    dtb = np.ascontiguousarray(np.tile(np.asarray(inputs["L0_attn_dt_bias"], dtype=f32), NTT)[None])
    alog = np.ascontiguousarray(np.tile(np.asarray(inputs["L0_attn_A_log"], dtype=f32), NTT)[None])
    drow = np.ascontiguousarray(np.asarray(inputs["L0_attn_D"], dtype=f32)[None])
    nw_t = np.ascontiguousarray(np.asarray(inputs["L0_attn_norm_weight"], dtype=f32).reshape(32, 128).T)
    ident = np.eye(128, dtype=f32)
    cumut = np.ascontiguousarray(np.triu(np.ones((128, 128), dtype=f32)))
    slo4 = np.ascontiguousarray(np.tile(
        (np.arange(128)[:, None] > np.arange(128)[None, :]).astype(f32), (1, 4)))
    negid = np.ascontiguousarray(30000.0 * np.eye(128, dtype=f32))
    e01 = np.ascontiguousarray(np.kron(np.eye(64, dtype=f32), np.ones((1, 64), dtype=f32)))
    return dict(wip=wip, convd=convd, convb=convb, wop=wop, wg=wg, wu=wu, wd=wd,
                dtb_t=dtb, alog_t=alog, d_row=drow, nw_t=nw_t,
                ident_in=ident, cumut_in=cumut, e01_in=e01,
                slo4_in=slo4, negid_in=negid)


def make_in_maps(inputs):
    shared = _prep_shared(inputs)
    x = np.asarray(inputs["L0_attn_hidden_states"], dtype=np.float32)
    maps = []
    for c in range(8):
        b, h = c // 2, c % 2
        xt = x[b, h * T:(h + 1) * T]
        if h == 0:
            ext = np.concatenate([np.zeros((3, HID), np.float32), xt], 0)
        else:
            ext = np.concatenate([x[b, h * T - 3:h * T], xt], 0)
        if h == 1:
            prev = np.concatenate([np.zeros((3, HID), np.float32), x[b, 0:T]], 0)
        else:
            prev = np.zeros((TE, HID), np.float32)
        m = dict(shared)
        m["x_fm"] = np.ascontiguousarray(ext.T)
        m["x_prev"] = np.ascontiguousarray(prev.T)
        m["x_tok"] = np.ascontiguousarray(xt)
        m["selm"] = np.full((128, 1), 1.0 if h == 1 else 0.0, np.float32)
        maps.append(m)
    return maps


def kernel(**inputs):
    nc, _ = build_nc()
    maps = make_in_maps(inputs)
    res = run_bass_kernel_spmd(nc, maps, core_ids=list(range(8)))
    out = np.empty((4, 2048, HID), np.float32)
    for c in range(8):
        b, h = c // 2, c % 2
        out[b, h * T:(h + 1) * T] = np.asarray(res.results[c]["out"], np.float32).T
    return out


# version 09
# speedup vs baseline: 1.0880x; 1.0880x over previous
import contextlib
import numpy as np
import ml_dtypes
import concourse.bass as bass
import concourse.mybir as mybir
from concourse.bass_utils import run_bass_kernel_spmd

F32 = mybir.dt.float32
I32 = mybir.dt.int32
BF16 = mybir.dt.bfloat16
AF = mybir.ActivationFunctionType
ALU = mybir.AluOpType
AX = mybir.AxisListType

HID = 2048
INTER = 4096
NH, HD, N = 64, 64, 128
CONV = 4352
PROJ = 8512
T = 1024
TE = T + 3
NKC = HID // 128
NTT = T // 128
NPAN = 17
C_MAGIC = float(np.float32(3 * 2 ** 22))
INV_DM = 1.0 / 2048.0
INV_INTER = 1.0 / 4096.0

ENGINES = ("tensor", "vector", "scalar", "gpsimd", "sync")
USE_COLLECTIVE = True
RG = [[0, 1], [2, 3], [4, 5], [6, 7]]


K_DMA = 8


class _Op:
    __slots__ = ("eng", "fn", "reads", "writes", "idx", "is_dma", "milestone",
                 "waits", "val_after", "dep_ops", "tag", "semkey")


class Prog:

    def __init__(self):
        self.ops = []
        self.last_write = {}
        self.readers = {}
        self.fences = []
        self.dma_count = {}

    def _add(self, eng, fn, reads, writes, is_dma, tag):
        op = _Op()
        op.eng, op.fn = eng, fn
        op.reads, op.writes = tuple(reads), tuple(writes)
        op.is_dma = is_dma
        op.milestone = is_dma
        op.waits = []
        op.tag = tag
        op.idx = len(self.ops)
        if is_dma:
            n = self.dma_count.get(eng, 0)
            self.dma_count[eng] = n + 1
            op.semkey = (eng, n % K_DMA)
        else:
            op.semkey = eng
        deps = {}
        for r in op.reads:
            w = self.last_write.get(r)
            if w is not None:
                deps[w.idx] = w
        for w_ in op.writes:
            lw = self.last_write.get(w_)
            if lw is not None:
                deps[lw.idx] = lw
            for rd in self.readers.get(w_, ()):
                deps[rd.idx] = rd
        op.dep_ops = [d for d in deps.values() if d.idx != op.idx
                      and not (d.eng == "tensor" and eng == "tensor")]
        for d in op.dep_ops:
            d.milestone = True
        self.ops.append(op)
        for r in op.reads:
            self.readers.setdefault(r, []).append(op)
        for w_ in op.writes:
            self.last_write[w_] = op
            self.readers[w_] = []
        return op

    def t(self, fn, reads, writes, tag=""):
        return self._add("tensor", fn, reads, writes, False, tag)

    def v(self, fn, reads, writes, tag=""):
        return self._add("vector", fn, reads, writes, False, tag)

    def s(self, fn, reads, writes, tag=""):
        return self._add("scalar", fn, reads, writes, False, tag)

    def pool(self, fn, reads, writes, tag=""):
        return self._add("gpsimd", fn, reads, writes, False, tag)

    def dma(self, q, fn, reads, writes, tag=""):
        return self._add(q, fn, reads, writes, True, tag)

    def cc(self, q, fn, reads, writes, tag=""):
        op = self._add(q, fn, reads, writes, True, tag)
        self.dma_count[q] -= 1
        op.semkey = (q, "cc")
        return op

    def fence(self):
        last = {}
        for o in self.ops:
            last[o.semkey] = o
        for o in last.values():
            o.milestone = True
        self.fences.append((len(self.ops), dict(last)))

    def resolve(self):
        cnt = {}
        for op in self.ops:
            if op.milestone:
                cnt[op.semkey] = cnt.get(op.semkey, 0) + (16 if op.is_dma else 1)
            op.val_after = cnt.get(op.semkey, 0)
        seen = {}
        fence_ptr = {e: 0 for e in ENGINES}
        for op in self.ops:
            need = {}
            fp = fence_ptr[op.eng]
            while fp < len(self.fences) and self.fences[fp][0] <= op.idx:
                for d in self.fences[fp][1].values():
                    if d.eng != op.eng or d.is_dma:
                        need[d.semkey] = max(need.get(d.semkey, 0), d.val_after)
                fp += 1
            fence_ptr[op.eng] = fp
            for d in op.dep_ops:
                need[d.semkey] = max(need.get(d.semkey, 0), d.val_after)
            if op.is_dma and op.val_after > 16:
                need[op.semkey] = max(need.get(op.semkey, 0), op.val_after - 16)
            for sk in sorted(need, key=repr):
                v = need[sk]
                if v > seen.get((op.eng, sk), 0):
                    op.waits.append((sk, v))
                    seen[(op.eng, sk)] = v

    def emit(self, nc):
        self.resolve()
        semkeys = {op.semkey for op in self.ops if op.milestone}
        for op in self.ops:
            for sk, _ in op.waits:
                semkeys.add(sk)
        sems = {}
        for sk in sorted(semkeys, key=repr):
            nm = sk if isinstance(sk, str) else f"{sk[0]}_d{sk[1]}"
            sems[sk] = nc.semaphore("sem_" + nm).__enter__()
        per = {e: [o for o in self.ops if o.eng == e] for e in ENGINES}

        def make_runner(ename):
            def run(eng):
                for op in per[ename]:
                    for (sk, v) in op.waits:
                        eng.wait_ge(sems[sk], v)
                    inst = op.fn(eng)
                    if op.milestone:
                        inst.then_inc(sems[op.semkey], 16 if op.is_dma else 1)
            return run

        with nc.Block() as block:
            block.tensor(make_runner("tensor"))
            block.vector(make_runner("vector"))
            block.scalar(make_runner("scalar"))
            block.gpsimd(make_runner("gpsimd"))
            block.sync(make_runner("sync"))


def build_nc(taps=()):
    taps = set(taps)
    nc = bass.Bass("TRN2", target_bir_lowering=False)
    p = Prog()
    es = contextlib.ExitStack()
    ec = es.enter_context

    def sb(name, shape, dt):
        return ec(nc.sbuf_tensor(name, shape, dt))

    def di(name, shape):
        return nc.dram_tensor(name, list(shape), F32, kind="ExternalInput").ap()

    x_fm_d = di("x_fm", [HID, TE])
    x_prev_d = di("x_prev", [HID, TE])
    x_tok_d = di("x_tok", [T, HID])
    wip_d = di("wip", [NPAN, NKC, 128, 512])
    convd_d = di("convd", [34, 4, 128, 128])
    convb_d = di("convb", [128, 34])
    wop_d = di("wop", [4, 32, 128, 512])
    wg_d = di("wg", [16, NKC, 128, 512])
    wu_d = di("wu", [16, NKC, 128, 512])
    wd_d = di("wd", [4, 64, 128, 512])
    dtb_d = di("dtb_t", [1, 512])
    alog_d = di("alog_t", [1, 512])
    drow_d = di("d_row", [1, 64])
    nw_d = di("nw_t", [128, 32])
    selm_d = di("selm", [128, 1])
    ident_d = di("ident_in", [128, 128])
    cumut_d = di("cumut_in", [128, 128])
    e01_d = di("e01_in", [64, INTER])
    slo4_d = di("slo4_in", [128, 512])
    negid_d = di("negid_in", [128, 128])

    out_d = nc.dram_tensor("out", [HID, T], F32, kind="ExternalOutput").ap()
    tap_d = {}

    def tapt(name, shape, dt=F32):
        if name in taps and name not in tap_d:
            tap_d[name] = nc.dram_tensor("tap_" + name, list(shape), dt,
                                         kind="ExternalOutput").ap()
        return tap_d.get(name)

    gate_dram = nc.dram_tensor("gate_dram", [8, NTT, 128, 512], BF16, kind="Internal").ap()
    flat_dram = nc.dram_tensor("flat_dram", [NTT, 6, 64, 128], BF16, kind="Internal").ap()
    mix_dram = nc.dram_tensor("mix_dram", [NTT, 128, HID], F32, kind="Internal").ap()
    part_dram = nc.dram_tensor("part_dram", [16, 128, T], F32, kind="Internal").ap()
    sinit_dram = nc.dram_tensor("sinit_dram", [N, INTER], F32, kind="Internal").ap()

    ident = sb("ident", [128, 128], BF16)
    slo4 = sb("slo4", [128, 512], BF16)
    negid = sb("negid", [128, 128], BF16)
    cumut = sb("cumut", [128, 128], F32)
    mask01 = sb("mask01", [128, 128], BF16)
    ones_col = sb("ones_col", [128, 1], F32)
    ones_r32 = sb("ones_r32", [1, 128], F32)
    ones_rb = sb("ones_rb", [1, 128], BF16)
    eps5 = sb("eps5", [128, 1], F32)
    eps8 = sb("eps8", [128, 1], F32)
    selm = sb("selm_sb", [128, 1], F32)
    nw_sb = sb("nw_sb", [128, 32], F32)
    convb_sb = sb("convb_sb", [128, 34], F32)
    B_fm = sb("B_fm", [128, T], BF16)
    C_fm = sb("C_fm", [128, T], BF16)
    BT = sb("BT", [128, T], BF16)
    w_fm = sb("w_fm", [64, NTT * 128], BF16)
    ea_fm = sb("ea_fm", [64, NTT * 128], BF16)
    lam_cr = sb("lam_cr", [64, NTT * 128], BF16)
    d_cr = sb("d_cr", [64, 128], BF16)
    lndt = sb("lndt", [128, 512], F32)
    stat = sb("stat", [128, 16], F32)
    sums8 = sb("sums8", [128, 8], F32)

    XT = sb("XT", [128, NTT * INTER], BF16)
    AR2 = sb("AR2", [128, 16448], BF16)
    AR3 = sb("AR3", [128, 16384], BF16)
    AR4 = sb("AR4", [128, 16384], BF16)
    AR5 = sb("AR5", [128, 14464], BF16)

    def cv(arena, off_b, width, dt):
        if dt == BF16:
            return arena[:, off_b // 2: off_b // 2 + width]
        return arena[:, off_b // 2: off_b // 2 + width * 2].bitcast(F32)

    x_fm_v = cv(AR2, 0, NKC * TE, BF16)
    wpan = [cv(AR3, 0, 8192, BF16), cv(AR3, 16384, 8192, BF16)]
    hbc_v = cv(AR4, 0, 2 * TE, BF16)
    hs_v = cv(AR4, 4160, T, BF16)
    gate_st = cv(AR4, 6208, 1024, BF16)
    convd_v = [cv(AR4, 8256, 512, BF16), cv(AR4, 9280, 512, BF16)]
    dt_tok = cv(AR5, 0, 512, F32)
    sp_dt = cv(AR5, 2048, 512, F32)
    ahat = cv(AR5, 4096, 512, F32)
    t_ab = cv(AR5, 6144, 512, F32)
    dta = cv(AR5, 8192, 512, F32)
    wsp = cv(AR5, 10240, 512, F32)
    hilo_off = 12288
    hilo = {k: cv(AR5, hilo_off + i * 1024, 512, BF16)
            for i, k in enumerate(("hi", "lo", "hi_n", "lo_n", "lnhi", "lnlo"))}
    stage64 = cv(AR5, 18432, 128, BF16)
    wsb_bf = cv(AR5, 18688, 512, BF16)
    lam_row = cv(AR5, 19712, 512, BF16)[0:1, :]
    dtb_bc = cv(AR5, 20736, 512, F32)
    arep_bc = cv(AR5, 22784, 512, F32)
    rbuf = cv(AR5, 24832, 512, F32)[0:1, :]
    arow = cv(AR5, 26880, 512, F32)[0:1, :]
    y1 = cv(AR5, 12288, 512, F32)
    t_m = cv(AR5, 14336, 512, F32)
    ea_g = cv(AR5, 16384, 512, BF16)
    g0rep = cv(AR5, 17408, 512, BF16)
    ghat_v = [cv(AR5, 18432, 512, BF16), cv(AR5, 19456, 512, BF16)]
    dmin_v = cv(AR5, 20480, 512, F32)
    hsd = cv(AR5, 0, 512, F32)
    sdec = cv(AR5, 2048, 512, F32)
    sjunk = cv(AR5, 6144, 512, F32)
    sinit_v = cv(AR5, 8192, 512, F32)
    S32 = cv(AR4, 0, INTER, F32)
    Sbf = cv(AR4, 16384, INTER, BF16)
    XW = cv(AR4, 24576, INTER, BF16)
    sg_v = cv(AR3, 0, INTER, BF16)
    drep = cv(AR3, 8192, INTER, BF16)
    e01 = AR3[0:64, 8192:12288]
    dmatL = AR2[:, 0:8192]
    dmatR = AR2[:, 8192:16384]
    r1fm = AR2[:, 0:NKC * T]
    xqT = AR3[:, 0:NKC * T]
    mixtt = cv(AR5, 0, HID, F32)
    xts = cv(AR5, 8192, HID, F32)
    zbuf = cv(AR4, 0, HID, F32)
    xqbf = cv(AR4, 8192, HID, BF16)
    r1bf = cv(AR4, 12288, HID, BF16)
    wopb = [cv(AR4, 16384 + i * 1024, 512, BF16) for i in range(2)]
    cjunk = cv(AR5, 16384, HID, F32)
    wgb = AR4[:, 0:8192]
    wub = AR4[:, 8192:16384]
    p_fm = XT[:]
    wdb = [cv(AR5, 0, 512, BF16), cv(AR5, 1024, 512, BF16),
           cv(AR5, 27648, 512, BF16), cv(AR5, 19456, 512, BF16)]
    sgev = cv(AR5, 2048, 512, BF16)
    partst = cv(AR5, 3072, T, F32)
    r2c = cv(AR5, 7168, T, F32)
    sqc = cv(AR5, 11264, T, F32)
    outc = cv(AR5, 15360, T, F32)
    meanbc = cv(AR5, 19456, T, F32)
    rstdbc = cv(AR5, 23552, T, F32)

    pb = [ec(nc.psum_tensor(f"pb{i}", [128, 512], F32)) for i in range(8)]
    PB = [f"pb{i}" for i in range(8)]

    p.dma("gpsimd", lambda g: g.dma_start(out=ident[:], in_=ident_d), ["d"], ["ident"])
    p.dma("sync", lambda g: g.dma_start(out=cumut[:], in_=cumut_d), ["d"], ["cumut"])
    p.dma("gpsimd", lambda g: g.dma_start(out=mask01[:], in_=cumut_d), ["d"], ["mask01"])
    p.dma("gpsimd", lambda g: g.dma_start(out=slo4[:], in_=slo4_d), ["d"], ["slo4"])
    p.dma("gpsimd", lambda g: g.dma_start(out=negid[:], in_=negid_d), ["d"], ["negid"])
    p.dma("sync", lambda g: g.dma_start(out=selm[:], in_=selm_d), ["d"], ["selm"])
    p.dma("sync", lambda g: g.dma_start(out=convb_sb[:], in_=convb_d), ["d"], ["convb"])
    p.dma("sync", lambda g: g.dma_start(out=nw_sb[:], in_=nw_d), ["d"], ["nw"])
    p.v(lambda v: v.memset(ones_col[:], 1.0), [], ["ones_col"])
    p.v(lambda v: v.memset(ones_r32[:], 1.0), [], ["ones_r32"])
    p.v(lambda v: v.memset(ones_rb[:], 1.0), [], ["ones_rb"])
    p.v(lambda v: v.memset(eps5[:], 1e-5), [], ["eps5"])
    p.v(lambda v: v.memset(eps8[:], 1e-8), [], ["eps8"])

    def load_x(src):
        p.dma("gpsimd",
              lambda g: g.dma_start(out=x_fm_v.rearrange("p (k t) -> p k t", k=NKC),
                                    in_=src.rearrange("(k p) t -> p k t", p=128)),
              ["d"], ["AR2"], tag="ldx")

    p.dma("sync", lambda g: g.dma_start(out=rbuf[:], in_=dtb_d), ["d"], ["rbuf"])
    p.t(lambda t: t.matmul(pb[0][:, :], ones_r32[:], rbuf[:], start=True, stop=True),
        ["ones_r32", "rbuf"], [PB[0]])
    p.v(lambda v: v.tensor_copy(dtb_bc[:], pb[0][:, :]), [PB[0]], ["dtb_bc"])
    p.dma("sync", lambda g: g.dma_start(out=rbuf[:], in_=alog_d), ["rbuf", "dtb_bc"], ["rbuf"])
    p.t(lambda t: t.matmul(pb[0][:, :], ones_r32[:], rbuf[:], start=True, stop=True),
        ["ones_r32", "rbuf"], [PB[0]])
    p.s(lambda s: s.activation(arep_bc[:], pb[0][:, :], AF.Exp), [PB[0]], ["arep_bc"])
    p.v(lambda v: v.tensor_scalar(arep_bc[:], arep_bc[:], -1.0, None, ALU.mult),
        ["arep_bc"], ["arep_bc"])
    drw = sb("drw", [1, 64], BF16)
    p.dma("gpsimd", lambda g: g.dma_start(out=drw[:], in_=drow_d), ["d"], ["drw"])
    p.t(lambda t: t.matmul(pb[1][0:64, 0:128], drw[:], ones_rb[:], start=True, stop=True),
        ["drw", "ones_rb"], [PB[1]])
    p.v(lambda v: v.tensor_copy(d_cr[:], pb[1][0:64, 0:128]), [PB[1]], ["d_cr"])

    def load_panel(pan):
        i = pan % 2
        keys = [f"wpan{i}"] + (["e01"] if i == 1 else [])
        p.dma("gpsimd",
              lambda g, pan=pan, i=i: g.dma_start(
                  out=wpan[i].rearrange("p (k j) -> p k j", k=NKC),
                  in_=wip_d[pan].rearrange("k p j -> p k j")),
              ["d"], keys, tag=f"ldpan{pan}")

    XTr = XT[:].rearrange("p (tt c) -> p tt c", tt=NTT)

    def phase_a(full):
      pans = list(range(NPAN)) if full else list(range(8, NPAN))
      load_panel(pans[0])
      load_panel(pans[1])
      p.dma("gpsimd", lambda g: g.dma_start(out=convd_v[0].rearrange("p (k j) -> p k j", k=4),
                                            in_=convd_d[0].rearrange("k p j -> p k j")),
            ["d"], ["convd0"], tag="ldcv")
      for pi, pan in enumerate(pans):
        i = pan % 2
        wk = f"wpan{i}"
        wv = wpan[i].rearrange("p (k j) -> p k j", k=NKC)
        if pan < 8:
            for tt in range(NTT):
                bank = tt % 2
                for kc in range(NKC):
                    p.t(lambda t, kc=kc, tt=tt, bank=bank, wv=wv:
                        t.matmul(pb[bank][:, :],
                                 x_fm_v[:, kc * TE + 3 + tt * 128: kc * TE + 3 + (tt + 1) * 128],
                                 wv[:, kc, :], start=(kc == 0), stop=(kc == NKC - 1)),
                        ["AR2", wk], [PB[bank]], tag=f"gmm{pan}_{tt}")
                p.s(lambda s, bank=bank: s.copy(gate_st[:, bank * 512:(bank + 1) * 512], pb[bank][:, :]),
                    [PB[bank]], [f"gst{bank}"], tag="gevac")
                p.dma("sync",
                      lambda g, pan=pan, tt=tt, bank=bank: g.dma_start(
                          out=gate_dram[pan, tt], in_=gate_st[:, bank * 512:(bank + 1) * 512]),
                      [f"gst{bank}"], [f"g{pan}_{tt}"], tag="gspill")
                if tapt("gate", [8, NTT, 128, 512], BF16) is not None:
                    p.dma("sync",
                          lambda g, pan=pan, tt=tt, bank=bank: g.dma_start(
                              out=tap_d["gate"][pan, tt], in_=gate_st[:, bank * 512:(bank + 1) * 512]),
                          [f"gst{bank}"], ["OUTD"])
        else:
            ocs = [(pan - 8) * 4 + j for j in range(4)]
            ocs = [oc for oc in ocs if oc < 34]
            for oc in ocs:
                slot = oc % 2
                cslot = oc % 2
                nxt = oc + 1
                if nxt < 34:
                    p.dma("gpsimd", lambda g, nxt=nxt: g.dma_start(
                        out=convd_v[nxt % 2].rearrange("p (k j) -> p k j", k=4),
                        in_=convd_d[nxt].rearrange("k p j -> p k j")),
                        ["d"], [f"convd{nxt % 2}"], tag="ldcv")
                for (j0, j1, bank) in ((0, 512, 0), (512, 1024, 1), (1024, TE, 2)):
                    for kc in range(NKC):
                        p.t(lambda t, kc=kc, oc=oc, j0=j0, j1=j1, bank=bank, wv=wv, pan=pan:
                            t.matmul(pb[bank][:, 0:j1 - j0],
                                     wv[:, kc, (oc - (pan - 8) * 4) * 128:(oc - (pan - 8) * 4 + 1) * 128],
                                     x_fm_v[:, kc * TE + j0: kc * TE + j1],
                                     start=(kc == 0), stop=(kc == NKC - 1)),
                            ["AR2", wk], [PB[bank]], tag=f"hmm{oc}")
                    p.s(lambda s, slot=slot, j0=j0, j1=j1, bank=bank:
                        s.copy(hbc_v[:, slot * TE + j0: slot * TE + j1], pb[bank][:, 0:j1 - j0]),
                        [PB[bank]], [f"hbc{slot}"], tag="hevac")
                for (j0, bank) in ((0, 3), (512, 4)):
                    for k in range(4):
                        p.t(lambda t, oc=oc, slot=slot, j0=j0, k=k, bank=bank, cslot=cslot:
                            t.matmul(pb[bank][:, :],
                                     convd_v[cslot][:, k * 128:(k + 1) * 128],
                                     hbc_v[:, slot * TE + j0 + k: slot * TE + j0 + k + 512],
                                     start=(k == 0), stop=(k == 3)),
                            [f"hbc{slot}", f"convd{cslot}"], [PB[bank]], tag=f"cvmm{oc}")
                    if oc < 32:
                        p.s(lambda s, oc=oc, j0=j0, bank=bank:
                            s.activation(hs_v[:, j0:j0 + 512], pb[bank][:, :], AF.Silu,
                                         bias=convb_sb[:, oc:oc + 1]),
                            [PB[bank], "convb"], ["hs"], tag="silu_hs")
                    elif oc == 32:
                        p.s(lambda s, j0=j0, bank=bank:
                            s.activation(B_fm[:, j0:j0 + 512], pb[bank][:, :], AF.Silu,
                                         bias=convb_sb[:, 32:33]),
                            [PB[bank], "convb"], ["B_fm"], tag="silu_B")
                    else:
                        p.s(lambda s, j0=j0, bank=bank:
                            s.activation(C_fm[:, j0:j0 + 512], pb[bank][:, :], AF.Silu,
                                         bias=convb_sb[:, 33:34]),
                            [PB[bank], "convb"], ["C_fm"], tag="silu_C")
                if oc < 32:
                    for half in range(2):
                        bank = 5 + half
                        for q in range(4):
                            tt = half * 4 + q
                            p.t(lambda t, tt=tt, q=q, bank=bank:
                                t.matmul(pb[bank][:, q * 128:(q + 1) * 128],
                                         hs_v[:, tt * 128:(tt + 1) * 128], ident[:],
                                         start=(q == 0), stop=(q == 3)),
                                ["hs", "ident"], [PB[bank]], tag=f"trp{oc}")
                        p.v(lambda v, half=half, oc=oc, bank=bank:
                            v.tensor_copy(
                                XTr[:, half * 4:(half + 1) * 4, oc * 128:(oc + 1) * 128],
                                pb[bank][:, :].rearrange("p (q c) -> p q c", q=4)),
                            [PB[bank]], ["XT"], tag="trpev")
            if pan == 16:
                for tt in range(NTT):
                    for kc in range(NKC):
                        p.t(lambda t, kc=kc, tt=tt, wv=wv:
                            t.matmul(pb[7][:, tt * 64:(tt + 1) * 64],
                                     x_fm_v[:, kc * TE + 3 + tt * 128: kc * TE + 3 + (tt + 1) * 128],
                                     wv[:, kc, 256:320],
                                     start=(kc == 0 and tt == 0), stop=(kc == NKC - 1 and tt == NTT - 1)),
                            ["AR2", wk], [PB[7]], tag="dtmm")
                p.v(lambda v: v.tensor_copy(dt_tok[:], pb[7][:, :]), [PB[7]], ["dt_tok"])
        if pi + 2 < len(pans):
            load_panel(pans[pi + 2])
      for tt in range(NTT):
        bank = 5 + (tt % 2)
        p.t(lambda t, tt=tt, bank=bank:
            t.matmul(pb[bank][:, 0:128], B_fm[:, tt * 128:(tt + 1) * 128], ident[:],
                     start=True, stop=True),
            ["B_fm", "ident"], [PB[bank]], tag="bT")
        p.v(lambda v, tt=tt, bank=bank:
            v.tensor_copy(BT[:, tt * 128:(tt + 1) * 128], pb[bank][:, 0:128]),
            [PB[bank]], ["BT"])
      if full:
        if tapt("XT", [128, NTT * INTER], BF16) is not None:
            p.dma("sync", lambda g: g.dma_start(out=tap_d["XT"], in_=XT[:]), ["XT"], ["OUTD"])
        if tapt("B", [128, T], BF16) is not None:
            p.dma("sync", lambda g: g.dma_start(out=tap_d["B"], in_=B_fm[:]), ["B_fm"], ["OUTD"])
        if tapt("C", [128, T], BF16) is not None:
            p.dma("sync", lambda g: g.dma_start(out=tap_d["C"], in_=C_fm[:]), ["C_fm"], ["OUTD"])
        if tapt("dt_raw", [128, 512]) is not None:
            p.dma("sync", lambda g: g.dma_start(out=tap_d["dt_raw"], in_=dt_tok[:]), ["dt_tok"], ["OUTD"])

    def scan_prep(full):
      p.v(lambda v: v.tensor_add(sp_dt[:], dt_tok[:], dtb_bc[:]), ["dt_tok", "dtb_bc"], ["sp_dt"])
      p.s(lambda s: s.activation(t_ab[:], sp_dt[:], AF.Abs), ["sp_dt"], ["t_ab"])
      p.s(lambda s: s.activation(t_ab[:], t_ab[:], AF.Exp, scale=-1.0), ["t_ab"], ["t_ab"])
      p.s(lambda s: s.activation(t_ab[:], t_ab[:], AF.Ln, bias=ones_col[:, 0:1]), ["t_ab", "ones_col"], ["t_ab"])
      p.v(lambda v: v.tensor_scalar_max(sp_dt[:], sp_dt[:], 0.0), ["sp_dt"], ["sp_dt"])
      p.v(lambda v: v.tensor_add(sp_dt[:], sp_dt[:], t_ab[:]), ["sp_dt", "t_ab"], ["sp_dt"])
      p.s(lambda s: s.activation(lndt[:], sp_dt[:], AF.Ln), ["sp_dt"], ["lndt"])
      if full and tapt("dt", [128, 512]) is not None:
        p.dma("sync", lambda g: g.dma_start(out=tap_d["dt"], in_=sp_dt[:]), ["sp_dt"], ["OUTD"])
      p.v(lambda v: v.tensor_mul(dta[:], sp_dt[:], arep_bc[:]), ["sp_dt", "arep_bc"], ["dta"])
      p.t(lambda t: t.matmul(pb[0][:, :], cumut[:], dta[:], start=True, stop=True),
          ["cumut", "dta"], [PB[0]], tag="cumsum")
      p.v(lambda v: v.tensor_copy(ahat[:], pb[0][:, :]), [PB[0]], ["ahat"])
      if full and tapt("ahat", [128, 512]) is not None:
        p.dma("sync", lambda g: g.dma_start(out=tap_d["ahat"], in_=ahat[:]), ["ahat"], ["OUTD"])
      p.dma("gpsimd", lambda g: g.dma_start(out=arow[:], in_=ahat[127:128, :]), ["ahat"], ["arow"])
      p.t(lambda t: t.matmul(pb[1][:, :], ones_r32[:], arow[:], start=True, stop=True),
          ["ones_r32", "arow"], [PB[1]], tag="arowb")
      p.v(lambda v: v.tensor_sub(wsp[:], pb[1][:, :], ahat[:]), [PB[1], "ahat"], ["wsp"])
      p.v(lambda v: v.tensor_add(wsp[:], wsp[:], lndt[:]), ["wsp", "lndt"], ["wsp"])
      p.s(lambda s: s.activation(wsb_bf[:], wsp[:], AF.Exp), ["wsp"], ["wsb_bf"])
      if full:
        p.v(lambda v: v.tensor_scalar(lndt[:], lndt[:], -1.0, None, ALU.mult),
            ["lndt", "wsp"], ["lndt"], tag="negln")
      p.s(lambda s: s.activation(lam_row[:], arow[:], AF.Exp), ["arow"], ["lam_row"])
      ea_tmp = cv(AR5, 8192, 512, BF16)
      if full:
        p.s(lambda s: s.activation(ea_tmp[:], ahat[:], AF.Exp), ["ahat", "dta"], ["ea_tmp"])
      for tt in range(NTT):
        p.t(lambda t, tt=tt: t.matmul(pb[2][0:64, 0:128], lam_row[:, tt * 64:(tt + 1) * 64],
                                      ones_rb[:], start=True, stop=True),
            ["lam_row", "ones_rb"], [PB[2]], tag="lamcr")
        p.v(lambda v, tt=tt: v.tensor_copy(lam_cr[:, tt * 128:(tt + 1) * 128], pb[2][0:64, 0:128]),
            [PB[2]], ["lam_cr"])
      srcs = [("wsb_bf", wsb_bf, w_fm, "w_fm")]
      if full:
        srcs.append(("ea_tmp", ea_tmp, ea_fm, "ea_fm"))
      for (srck, src, dst, dstk) in srcs:
        for tt in range(NTT):
            bank = 3 + tt % 2
            p.t(lambda t, tt=tt, src=src, bank=bank:
                t.matmul(pb[bank][0:64, 0:128], src[:, tt * 64:(tt + 1) * 64], ident[:],
                         start=True, stop=True),
                [srck, "ident"], [PB[bank]], tag="wfm")
            p.v(lambda v, tt=tt, dst=dst, bank=bank, dstk=dstk:
                v.tensor_copy(dst[:, tt * 128:(tt + 1) * 128], pb[bank][0:64, 0:128]),
                [PB[bank]], [dstk])
      if not full:
        return
      neg_a = cv(AR5, 10240, 512, F32)
      p.v(lambda v: v.tensor_scalar(neg_a[:], ahat[:], -1.0, None, ALU.mult), ["ahat", "wsp"], ["neg_a"])
      for (name, src, srck) in (("hi", ahat, "ahat"), ("hi_n", neg_a, "neg_a"), ("lnhi", lndt, "lndt")):
        lo = {"hi": "lo", "hi_n": "lo_n", "lnhi": "lnlo"}[name]
        p.v(lambda v, name=name, src=src: v.tensor_copy(hilo[name][:], src[:]), [srck], ["hl" + name])
        p.v(lambda v, name=name, lo=lo, src=src: v.tensor_sub(hilo[lo][:], src[:], hilo[name][:]),
            [srck, "hl" + name], ["hl" + lo])
      ARRS = ["hi", "lo", "hi_n", "lo_n", "lnhi", "lnlo"]
      for tt in range(NTT):
        for ai, name in enumerate(ARRS):
            bank = 5 + (ai % 2)
            p.t(lambda t, tt=tt, name=name, bank=bank:
                t.matmul(pb[bank][0:64, 0:128], hilo[name][:, tt * 64:(tt + 1) * 64], ident[:],
                         start=True, stop=True),
                ["hl" + name, "ident"], [PB[bank]], tag="flT")
            p.v(lambda v, bank=bank: v.tensor_copy(stage64[0:64, 0:128], pb[bank][0:64, 0:128]),
                [PB[bank]], ["st64"], tag="flev")
            p.dma("sync", lambda g, tt=tt, ai=ai: g.dma_start(out=flat_dram[tt, ai], in_=stage64[0:64, 0:128]),
                  ["st64"], [f"fl{tt}_{ai}"], tag="fldma")

    def load_e01():
        p.dma("gpsimd", lambda g: g.dma_start(out=e01[:], in_=e01_d),
              ["d"], ["e01", "wpan1"], tag="lde01")

    def drep_build():
        for g_ in range(8):
            p.t(lambda t, g2=g_: t.matmul(pb[4][:, :], d_cr[:], e01[:, g2 * 512:(g2 + 1) * 512],
                                          start=True, stop=True),
                ["d_cr", "e01"], [PB[4]], tag="drep")
            p.v(lambda v, g2=g_: v.tensor_copy(drep[:, g2 * 512:(g2 + 1) * 512], pb[4][:, :]),
                [PB[4]], ["drep"])
        p.v(lambda v: v.memset(dmatL[0:6, :], 1.0), ["dmatL"], ["dmatL"])
        p.v(lambda v: v.memset(dmatR[0:6, :], 1.0), ["dmatR"], ["dmatR"])

    def load_delta(tt):
        p.dma("sync", lambda g, tt=tt: g.dma_start(
            out=dmatL[0:2, :].rearrange("a (h t) -> a h t", h=64),
            in_=flat_dram[tt, 0:2]), [f"fl{tt}_0", f"fl{tt}_1", "dmatL"], ["dmatL"], tag="dlL")
        p.dma("sync", lambda g, tt=tt: g.dma_start(
            out=dmatL[4:6, :].rearrange("a (h t) -> a h t", h=64),
            in_=flat_dram[tt, 4:6]), [f"fl{tt}_4", f"fl{tt}_5", "dmatL"], ["dmatL"], tag="dlL")
        p.dma("sync", lambda g, tt=tt: g.dma_start(
            out=dmatR[2:4, :].rearrange("a (h t) -> a h t", h=64),
            in_=flat_dram[tt, 2:4]), [f"fl{tt}_2", f"fl{tt}_3", "dmatR"], ["dmatR"], tag="drL")


    def state_chunk(tt, phase):
        for g_ in range(8):
            p.t(lambda t, tt=tt, g2=g_: t.matmul(pb[0][:, :], w_fm[:, tt * 128:(tt + 1) * 128],
                                                 e01[:, g2 * 512:(g2 + 1) * 512], start=True, stop=True),
                ["w_fm", "e01"], [PB[0]], tag=f"{phase}wx")
            p.v(lambda v, tt=tt, g2=g_: v.tensor_mul(XW[:, g2 * 512:(g2 + 1) * 512], pb[0][:, :],
                                                     XT[:, tt * INTER + g2 * 512: tt * INTER + (g2 + 1) * 512]),
                [PB[0], "XT"], ["XW"], tag=f"{phase}xw")
            p.t(lambda t, tt=tt, g2=g_: t.matmul(pb[1][:, :], BT[:, tt * 128:(tt + 1) * 128],
                                                 XW[:, g2 * 512:(g2 + 1) * 512], start=True, stop=True),
                ["BT", "XW"], [PB[1]], tag=f"{phase}ds")
            p.t(lambda t, tt=tt, g2=g_: t.matmul(pb[2][:, :], lam_cr[:, tt * 128:(tt + 1) * 128],
                                                 e01[:, g2 * 512:(g2 + 1) * 512], start=True, stop=True),
                ["lam_cr", "e01"], [PB[2]], tag=f"{phase}lx")
            p.v(lambda v, g2=g_: v.tensor_mul(sdec[:], S32[:, g2 * 512:(g2 + 1) * 512], pb[2][:, :]),
                ["S32", PB[2]], ["sdec"], tag=f"{phase}dec")
            p.v(lambda v, g2=g_: v.tensor_add(S32[:, g2 * 512:(g2 + 1) * 512], sdec[:], pb[1][:, :]),
                ["sdec", PB[1]], ["S32"], tag=f"{phase}sup")

    def b1_pass():
        p.v(lambda v: v.memset(S32[:], 0.0), ["AR4"], ["S32"])
        for tt in range(NTT):
            state_chunk(tt, "b1")
        if tapt("S8", [N, INTER]) is not None:
            p.dma("sync", lambda g: g.dma_start(out=tap_d["S8"], in_=S32[:]), ["S32"], ["OUTD"])
        p.v(lambda v: v.tensor_scalar(S32[:], S32[:], selm[:, 0:1], None, ALU.mult),
            ["S32", "selm"], ["S32"], tag="sinm")
        p.dma("sync", lambda g: g.dma_start(out=sinit_dram, in_=S32[:]), ["S32"], ["ccin"],
              tag="ccin")

    load_x(x_prev_d)
    phase_a(False)
    scan_prep(False)
    p.fence()
    load_e01()
    b1_pass()
    p.fence()

    load_x(x_fm_d)
    phase_a(True)
    scan_prep(True)
    p.fence()
    load_e01()
    drep_build()
    load_delta(0)
    p.dma("sync", lambda g: g.dma_start(out=S32[:], in_=sinit_dram),
          ["ccin", "S32"], ["S32"], tag="sinld")

    Sbf2 = cv(AR5, 20480, INTER, BF16)
    Sbf_v = [Sbf, Sbf2]
    y1_v = [y1, cv(AR4, 28672, 512, F32)]
    t_m_v = [t_m, cv(AR4, 30720, 512, F32)]
    ea_g_v = [ea_g, cv(AR5, 4096, 512, BF16)]
    hsd_v = [hsd, cv(AR5, 10240, 512, F32)]

    def b2_intra(tt):
        par = tt % 2
        sbc, sbk = Sbf_v[par], f"Sbf{par}"
        for pan in range(8):
            p.dma("sync", lambda g, tt=tt, pan=pan: g.dma_start(
                out=sg_v[:, pan * 512:(pan + 1) * 512], in_=gate_dram[pan, tt]),
                [f"g{pan}_{tt}", f"SG{pan}"], [f"SG{pan}"], tag="gld")
        p.s(lambda s: s.activation(sg_v[:], sg_v[:], AF.Silu),
            [f"SG{i}" for i in range(8)], [f"SG{i}" for i in range(8)], tag="sgsilu")
        p.t(lambda t, tt=tt: t.matmul(pb[3][:, 0:128], B_fm[:, tt * 128:(tt + 1) * 128],
                                      C_fm[:, tt * 128:(tt + 1) * 128], start=True, stop=True),
            ["B_fm", "C_fm"], [PB[3]], tag="g0")
        p.v(lambda v: v.tensor_mul(g0rep[:, 0:128], pb[3][:, 0:128], mask01[:]),
            [PB[3], "mask01"], ["G0R"], tag="g0m")
        for q in range(1, 4):
            p.s(lambda s, q=q: s.copy(g0rep[:, q * 128:(q + 1) * 128], g0rep[:, 0:128]),
                ["G0R"], ["G0R"], tag="g0rep")
        for hg in range(16):
            gb = hg % 2
            for j in range(4):
                h = hg * 4 + j
                p.t(lambda t, h=h, j=j, gb=gb:
                    t.matmul(pb[4 + gb][:, j * 128:(j + 1) * 128],
                             dmatL[0:6, h * 128:(h + 1) * 128],
                             dmatR[0:6, h * 128:(h + 1) * 128],
                             start=(j == 0), stop=False),
                    ["dmatL", "dmatR"], [PB[4 + gb]], tag="dmm")
            p.t(lambda t, gb=gb:
                t.matmul(pb[4 + gb][:, :], negid[:], slo4[:], start=False, stop=True),
                ["negid", "slo4"], [PB[4 + gb]], tag="dmmtri")
            p.s(lambda s, gb=gb: s.activation(ghat_v[gb][:], pb[4 + gb][:, :], AF.Exp,
                                              scale=-1.0),
                [PB[4 + gb]], [f"GH{gb}"], tag="eexp")
            p.v(lambda v, gb=gb: v.tensor_mul(ghat_v[gb][:], ghat_v[gb][:], g0rep[:]),
                [f"GH{gb}", "G0R"], [f"GH{gb}"], tag="ghat")
            for j in range(4):
                h = hg * 4 + j
                p.t(lambda t, tt=tt, h=h, j=j, gb=gb:
                    t.matmul(pb[6][:, (h % 8) * 64:((h % 8) + 1) * 64],
                             ghat_v[gb][:, j * 128:(j + 1) * 128],
                             XT[:, tt * INTER + h * 64: tt * INTER + (h + 1) * 64],
                             start=(h % 8 == 0 and j == 0), stop=(h % 8 == 7 and j == 3)),
                    [f"GH{gb}", "XT"], [PB[6]], tag="yi")
            if hg % 2 == 1:
                g_ = hg // 2
                pr = g_ % 2
                y1c, y1k = y1_v[pr], f"y1{pr}"
                tmc, tmk = t_m_v[pr], f"t_m{pr}"
                eac, eak = ea_g_v[pr], f"ea_g{pr}"
                hsc, hsk = hsd_v[pr], f"HSD{pr}"
                p.t(lambda t, tt=tt, g2=g_, sbc=sbc: t.matmul(pb[7][:, :], C_fm[:, tt * 128:(tt + 1) * 128],
                                                     sbc[:, g2 * 512:(g2 + 1) * 512], start=True, stop=True),
                    ["C_fm", sbk], [PB[7]], tag="yin")
                p.t(lambda t, tt=tt, g2=g_: t.matmul(pb[3][:, :], ea_fm[:, tt * 128:(tt + 1) * 128],
                                                     e01[:, g2 * 512:(g2 + 1) * 512], start=True, stop=True),
                    ["ea_fm", "e01"], [PB[3]], tag="eax")
                p.s(lambda s, eac=eac: s.copy(eac[:], pb[3][:, :]), [PB[3]], [eak], tag="eaev")
                p.v(lambda v, tmc=tmc, eac=eac: v.tensor_mul(tmc[:], pb[7][:, :], eac[:]),
                    [PB[7], eak], [tmk], tag="minter")
                p.v(lambda v, y1c=y1c, tmc=tmc: v.tensor_add(y1c[:], tmc[:], pb[6][:, :]),
                    [tmk, PB[6]], [y1k], tag="yadd")
                p.pool(lambda g, tt=tt, g2=g_, hsc=hsc: g.tensor_mul(hsc[:], XT[:, tt * INTER + g2 * 512: tt * INTER + (g2 + 1) * 512],
                                                            drep[:, g2 * 512:(g2 + 1) * 512]),
                       ["XT", "drep", hsk], [hsk], tag="dhs")
                p.pool(lambda g, y1c=y1c, hsc=hsc: g.tensor_add(y1c[:], y1c[:], hsc[:]),
                       [y1k, hsk], [y1k], tag="yadd2")
                p.v(lambda v, g2=g_, y1c=y1c: v.tensor_mul(sg_v[:, g2 * 512:(g2 + 1) * 512], y1c[:],
                                                  sg_v[:, g2 * 512:(g2 + 1) * 512]),
                    [y1k, f"SG{g_}"], [f"SG{g_}"], tag="ygate")
                p.s(lambda s, g2=g_: s.activation(sjunk[:], sg_v[:, g2 * 512:(g2 + 1) * 512], AF.Square,
                                                  accum_out=sums8[:, g2:g2 + 1]),
                    [f"SG{g_}", "SJ"], ["SJ", "sums8"], tag="sumsq")
        p.v(lambda v: v.tensor_reduce(stat[:, 0:1], sums8[:], AX.X, ALU.add),
            ["sums8", "stat"], ["stat"], tag="rsum")
        p.v(lambda v: v.tensor_scalar(stat[:, 1:2], stat[:, 0:1], INV_INTER, None, ALU.mult),
            ["stat"], ["stat"], tag="rvar")
        p.s(lambda s: s.activation(stat[:, 2:3], stat[:, 1:2], AF.Ln, bias=eps5[:, 0:1]),
            ["stat", "eps5"], ["stat"], tag="rln")
        p.s(lambda s: s.activation(stat[:, 3:4], stat[:, 2:3], AF.Exp, scale=-0.5),
            ["stat"], ["stat"], tag="rexp")
        for g_ in range(8):
            p.v(lambda v, g2=g_: v.tensor_scalar(sg_v[:, g2 * 512:(g2 + 1) * 512],
                                                 sg_v[:, g2 * 512:(g2 + 1) * 512], stat[:, 3:4],
                                                 None, ALU.mult),
                [f"SG{g_}", "stat"], [f"SG{g_}"], tag="ynorm")
        if tapt("yn", [NTT, 128, INTER], BF16) is not None:
            p.dma("sync", lambda g, tt=tt: g.dma_start(out=tap_d["yn"][tt], in_=sg_v[:]),
                  [f"SG{i}" for i in range(8)], ["OUTD"])
        for half in range(2):
            for qq in range(4):
                bank = 4 + half
                for q in range(4):
                    oc2 = half * 16 + qq * 4 + q
                    p.t(lambda t, oc2=oc2, q=q, bank=bank:
                        t.matmul(pb[bank][:, q * 128:(q + 1) * 128],
                                 sg_v[:, oc2 * 128:(oc2 + 1) * 128], ident[:],
                                 start=(q == 0), stop=(q == 3)),
                        [f"SG{oc2 // 4}", "ident"], [PB[bank]], tag="ynT")
                for q in range(4):
                    oc2 = half * 16 + qq * 4 + q
                    p.v(lambda v, tt=tt, oc2=oc2, q=q, bank=bank:
                        v.tensor_scalar(XT[:, tt * INTER + oc2 * 128: tt * INTER + (oc2 + 1) * 128],
                                        pb[bank][:, q * 128:(q + 1) * 128],
                                        nw_sb[:, oc2:oc2 + 1], None, ALU.mult),
                        [PB[bank], "nw", "XT"], ["XT"], tag="ynTev")
        if tt + 1 < NTT:
            load_delta(tt + 1)

    p.s(lambda s: s.copy(Sbf_v[0][:], S32[:]), ["S32"], ["Sbf0"], tag="sbf")
    state_chunk(0, "b2")
    for tt in range(NTT):
        if tt + 1 < NTT:
            par = (tt + 1) % 2
            p.s(lambda s, par=par: s.copy(Sbf_v[par][:], S32[:]), ["S32"], [f"Sbf{par}"], tag="sbf")
            state_chunk(tt + 1, "b2")
        b2_intra(tt)

    p.fence()

    def load_wop(jg, kc):
        i = kc % 2
        p.dma("gpsimd", lambda g, jg=jg, kc=kc, i=i: g.dma_start(out=wopb[i][:], in_=wop_d[jg, kc]),
              ["d", f"wopb{i}"], [f"wopb{i}"], tag="wopld")

    for jg in range(4):
        if jg == 0:
            load_wop(0, 0)
            load_wop(0, 1)
        for kc in range(32):
            i = kc % 2
            for tt in range(NTT):
                p.t(lambda t, jg=jg, kc=kc, tt=tt, i=i:
                    t.matmul(pb[tt][:, :], XT[:, tt * INTER + kc * 128: tt * INTER + (kc + 1) * 128],
                             wopb[i][:], start=(kc == 0), stop=(kc == 31)),
                    [f"wopb{i}", "XT"], [PB[tt]], tag="opmm")
            if kc + 2 < 32:
                load_wop(jg, kc + 2)
            elif jg + 1 < 4:
                load_wop(jg + 1, kc - 30)
        for tt in range(NTT):
            p.v(lambda v, jg=jg, tt=tt: v.tensor_copy(mixtt[:, 0:512], pb[tt][:, :]),
                [PB[tt], "MIXTT"], ["MIXTT"], tag="opev")
            p.dma("sync", lambda g, jg=jg, tt=tt: g.dma_start(out=mix_dram[tt][:, jg * 512:(jg + 1) * 512],
                                                              in_=mixtt[:, 0:512]),
                  ["MIXTT"], [f"mix{tt}_{jg}"], tag="opspill")

    def layer_norm(buf, bufk, dst, dstk, epsc, epsk, junk, junkk, sc):
        p.v(lambda v: v.tensor_reduce(stat[:, sc:sc + 1], buf[:], AX.X, ALU.add),
            [bufk, "stat"], ["stat"], tag="lred")
        p.s(lambda s: s.activation(junk[:], buf[:], AF.Square, accum_out=stat[:, sc + 1:sc + 2]),
            [bufk, junkk, "stat"], [junkk, "stat"], tag="lsq")
        p.v(lambda v: v.tensor_scalar(stat[:, sc:sc + 1], stat[:, sc:sc + 1], INV_DM, None, ALU.mult),
            ["stat"], ["stat"], tag="lm")
        p.v(lambda v: v.tensor_scalar(stat[:, sc + 1:sc + 2], stat[:, sc + 1:sc + 2], INV_DM, None, ALU.mult),
            ["stat"], ["stat"], tag="lm2")
        p.v(lambda v: v.tensor_mul(stat[:, sc + 2:sc + 3], stat[:, sc:sc + 1], stat[:, sc:sc + 1]),
            ["stat"], ["stat"], tag="lmm")
        p.v(lambda v: v.tensor_sub(stat[:, sc + 1:sc + 2], stat[:, sc + 1:sc + 2], stat[:, sc + 2:sc + 3]),
            ["stat"], ["stat"], tag="lvar")
        p.s(lambda s: s.activation(stat[:, sc + 2:sc + 3], stat[:, sc + 1:sc + 2], AF.Ln, bias=epsc[:, 0:1]),
            ["stat", epsk], ["stat"], tag="lln")
        p.s(lambda s: s.activation(stat[:, sc + 2:sc + 3], stat[:, sc + 2:sc + 3], AF.Exp, scale=-0.5),
            ["stat"], ["stat"], tag="lrexp")
        p.v(lambda v: v.scalar_tensor_tensor(stat[:, sc + 3:sc + 4], stat[:, sc:sc + 1], -1.0,
                                             stat[:, sc + 2:sc + 3], ALU.mult, ALU.mult),
            ["stat"], ["stat"], tag="lnb")
        p.v(lambda v: v.tensor_scalar(dst[:], buf[:], stat[:, sc + 2:sc + 3],
                                      stat[:, sc + 3:sc + 4], ALU.mult, ALU.add),
            [bufk, "stat", dstk], [dstk], tag="lapp")

    r1fmr = r1fm.rearrange("p (k t2) -> p k t2", k=NKC)
    xqTr = xqT.rearrange("p (k t2) -> p k t2", k=NKC)
    for tt in range(NTT):
        p.dma("sync", lambda g, tt=tt: g.dma_start(out=xts[:], in_=x_tok_d[tt * 128:(tt + 1) * 128, :]),
              ["d", "XTS"], ["XTS"], tag="xtld")
        p.dma("sync", lambda g, tt=tt: g.dma_start(out=mixtt[:], in_=mix_dram[tt]),
              [f"mix{tt}_{jg2}" for jg2 in range(4)] + ["MIXTT"], ["MIXTT"], tag="mixld")
        p.v(lambda v: v.tensor_add(mixtt[:], mixtt[:], xts[:]), ["MIXTT", "XTS"], ["MIXTT"], tag="resid1")
        layer_norm(mixtt, "MIXTT", mixtt, "MIXTT", eps5, "eps5", cjunk, "CJ", 4)
        if tapt("r1", [NTT, 128, HID]) is not None:
            p.dma("sync", lambda g, tt=tt: g.dma_start(out=tap_d["r1"][tt], in_=mixtt[:]),
                  ["MIXTT"], ["OUTD"])
        p.v(lambda v: v.tensor_copy(r1bf[:], mixtt[:]), ["MIXTT", "R1BF"], ["R1BF"], tag="r1bf")
        for quad in range(4):
            bank = quad % 2
            for q in range(4):
                kc = quad * 4 + q
                p.t(lambda t, kc=kc, q=q, bank=bank:
                    t.matmul(pb[bank][:, q * 128:(q + 1) * 128], r1bf[:, kc * 128:(kc + 1) * 128], ident[:],
                             start=(q == 0), stop=(q == 3)),
                    ["R1BF", "ident"], [PB[bank]], tag="r1T")
            p.v(lambda v, tt=tt, quad=quad, bank=bank:
                v.tensor_copy(r1fmr[:, quad * 4:(quad + 1) * 4, tt * 128:(tt + 1) * 128],
                              pb[bank][:, :].rearrange("p (q c) -> p q c", q=4)),
                [PB[bank], "R1FM"], ["R1FM"], tag="r1Tev")
        layer_norm(mixtt, "MIXTT", zbuf, "ZB", eps8, "eps8", cjunk, "CJ", 8)
        p.v(lambda v: v.reduce_max(stat[:, 12:13], zbuf[:], AX.X, apply_absolute_value=True),
            ["ZB", "stat"], ["stat"], tag="amax")
        p.v(lambda v: v.reciprocal(stat[:, 13:14], stat[:, 12:13]), ["stat"], ["stat"], tag="ramax")
        p.v(lambda v: v.tensor_scalar(stat[:, 13:14], stat[:, 13:14], 127.0, None, ALU.mult),
            ["stat"], ["stat"], tag="sfac")
        p.v(lambda v: v.tensor_scalar(stat[:, 14:15], stat[:, 12:13], 1.0 / 127.0, None, ALU.mult),
            ["stat"], ["stat"], tag="rsfac")
        p.v(lambda v: v.tensor_scalar(zbuf[:], zbuf[:], stat[:, 13:14], C_MAGIC, ALU.mult, ALU.add),
            ["ZB", "stat"], ["ZB"], tag="q1")
        p.v(lambda v: v.tensor_scalar(zbuf[:], zbuf[:], C_MAGIC, None, ALU.subtract),
            ["ZB"], ["ZB"], tag="q2")
        p.v(lambda v: v.tensor_scalar(xqbf[:], zbuf[:], stat[:, 14:15], None, ALU.mult),
            ["ZB", "stat", "XQB"], ["XQB"], tag="q3")
        if tapt("xq", [NTT, 128, HID], BF16) is not None:
            p.dma("sync", lambda g, tt=tt: g.dma_start(out=tap_d["xq"][tt], in_=xqbf[:]), ["XQB"], ["OUTD"])
        for quad in range(4):
            bank = 2 + quad % 2
            for q in range(4):
                kc = quad * 4 + q
                p.t(lambda t, kc=kc, q=q, bank=bank:
                    t.matmul(pb[bank][:, q * 128:(q + 1) * 128], xqbf[:, kc * 128:(kc + 1) * 128], ident[:],
                             start=(q == 0), stop=(q == 3)),
                    ["XQB", "ident"], [PB[bank]], tag="xqTm")
            p.v(lambda v, tt=tt, quad=quad, bank=bank:
                v.tensor_copy(xqTr[:, quad * 4:(quad + 1) * 4, tt * 128:(tt + 1) * 128],
                              pb[bank][:, :].rearrange("p (q c) -> p q c", q=4)),
                [PB[bank], "XQT"], ["XQT"], tag="xqTev")

    p.fence()

    wgbr = wgb.rearrange("p (k j) -> p k j", k=NKC)
    wubr = wub.rearrange("p (k j) -> p k j", k=NKC)

    def load_gu(pan, half):
        for which, d_, buf, bufr in (("WGH", wg_d, wgb, None), ("WUH", wu_d, wub, None)):
            dst = (wgb if which == "WGH" else wub)[:, half * 4096:(half + 1) * 4096]
            p.dma("gpsimd", lambda g, pan=pan, half=half, d_=d_, dst=dst: g.dma_start(
                out=dst.rearrange("p (k j) -> p k j", k=8),
                in_=d_[pan, half * 8:(half + 1) * 8].rearrange("k p j -> p k j")),
                ["d", f"{which}{half}"], [f"{which}{half}"], tag="ldgu")

    for Hhalf in range(2):
        pans = list(range(Hhalf * 8, (Hhalf + 1) * 8))
        load_gu(pans[0], 0)
        for pi, pan in enumerate(pans):
            load_gu(pan, 1)
            p.s(lambda s: s.sign(wgb[:, 0:4096], wgb[:, 0:4096]), ["WGH0"], ["WGH0"], tag="sgn")
            p.s(lambda s: s.sign(wub[:, 0:4096], wub[:, 0:4096]), ["WUH0"], ["WUH0"], tag="sgn")
            done_h1_sign = [False]

            def sign_h1():
                if not done_h1_sign[0]:
                    p.s(lambda s: s.sign(wgb[:, 4096:8192], wgb[:, 4096:8192]), ["WGH1"], ["WGH1"], tag="sgn")
                    p.s(lambda s: s.sign(wub[:, 4096:8192], wub[:, 4096:8192]), ["WUH1"], ["WUH1"], tag="sgn")
                    done_h1_sign[0] = True

            for oc in range(4):
                ho = (pi * 4 + oc)
                for jg in range(2):
                    for kc in range(NKC):
                        if kc == 8:
                            sign_h1()
                        kh = kc // 8
                        p.t(lambda t, oc=oc, kc=kc, jg=jg:
                            t.matmul(pb[jg][:, :], wgbr[:, kc, oc * 128:(oc + 1) * 128],
                                     xqTr[:, kc, jg * 512:(jg + 1) * 512],
                                     start=(kc == 0), stop=(kc == NKC - 1)),
                            [f"WGH{kh}", "XQT"], [PB[jg]], tag="gmmD")
                        p.t(lambda t, oc=oc, kc=kc, jg=jg:
                            t.matmul(pb[2 + jg][:, :], wubr[:, kc, oc * 128:(oc + 1) * 128],
                                     xqTr[:, kc, jg * 512:(jg + 1) * 512],
                                     start=(kc == 0), stop=(kc == NKC - 1)),
                            [f"WUH{kh}", "XQT"], [PB[2 + jg]], tag="ummD")
                    p.s(lambda s, jg=jg: s.activation(sgev[:], pb[jg][:, :], AF.Silu),
                        [PB[jg], "SGEV"], ["SGEV"], tag="gsilu")
                    p.v(lambda v, ho=ho, jg=jg: v.tensor_mul(p_fm[:, ho * T + jg * 512: ho * T + (jg + 1) * 512],
                                                             pb[2 + jg][:, :], sgev[:]),
                        [PB[2 + jg], "SGEV", "XT"], ["XT"], tag="pmul")
            if pi + 1 < len(pans):
                load_gu(pans[pi + 1], 0)
        if tapt("p_fm", [2, 128, 32 * T], BF16) is not None:
            p.dma("sync", lambda g, Hhalf=Hhalf: g.dma_start(out=tap_d["p_fm"][Hhalf], in_=XT[:]), ["XT"], ["OUTD"])

        def ldwd(gk, Hhalf=Hhalf):
            opan2, kci2 = gk // 32, gk % 32
            i = gk % 4
            p.dma("gpsimd", lambda g, opan2=opan2, kci2=kci2, i=i, Hhalf=Hhalf: g.dma_start(
                out=wdb[i][:], in_=wd_d[opan2, Hhalf * 32 + kci2]),
                [f"wdb{i}"], [f"wdb{i}"], tag="ldwd")
            p.s(lambda s, i=i: s.sign(wdb[i][:], wdb[i][:]), [f"wdb{i}"], [f"wdb{i}"], tag="sgnD")

        for gk0 in range(4):
            ldwd(gk0)
        for opan in range(4):
            for kci in range(32):
                gk = opan * 32 + kci
                i = gk % 4
                for oc in range(4):
                    for jg in range(2):
                        p.t(lambda t, i=i, kci=kci, oc=oc, jg=jg:
                            t.matmul(pb[oc * 2 + jg][:, :], wdb[i][:, oc * 128:(oc + 1) * 128],
                                     p_fm[:, kci * T + jg * 512: kci * T + (jg + 1) * 512],
                                     start=(kci == 0), stop=(kci == 31)),
                            [f"wdb{i}", "XT"], [PB[oc * 2 + jg]], tag="dmmD")
                if gk + 4 < 128:
                    ldwd(gk + 4)
            for oc in range(4):
                occ = opan * 4 + oc
                if Hhalf == 0:
                    p.v(lambda v, oc=oc: v.tensor_copy(partst[:, 0:512], pb[oc * 2][:, :]),
                        [PB[oc * 2], "PARTST"], ["PARTST"], tag="pev")
                    p.v(lambda v, oc=oc: v.tensor_copy(partst[:, 512:1024], pb[oc * 2 + 1][:, :]),
                        [PB[oc * 2 + 1], "PARTST"], ["PARTST"], tag="pev")
                    p.dma("sync", lambda g, occ=occ: g.dma_start(out=part_dram[occ], in_=partst[:]),
                          ["PARTST"], [f"part{occ}"], tag="pspill")
                else:
                    p.dma("sync", lambda g, occ=occ: g.dma_start(out=partst[:], in_=part_dram[occ]),
                          [f"part{occ}", "PARTST"], ["PARTST"], tag="pld")
                    p.v(lambda v, oc=oc: v.tensor_add(r2c[:, 0:512], pb[oc * 2][:, :], partst[:, 0:512]),
                        [PB[oc * 2], "PARTST", "R2C"], ["R2C"], tag="r2a")
                    p.v(lambda v, oc=oc: v.tensor_add(r2c[:, 512:1024], pb[oc * 2 + 1][:, :], partst[:, 512:1024]),
                        [PB[oc * 2 + 1], "PARTST", "R2C"], ["R2C"], tag="r2b")
                    p.v(lambda v, occ=occ: v.tensor_add(r2c[:], r2c[:], r1fm[:, occ * T:(occ + 1) * T]),
                        ["R2C", "R1FM"], ["R2C"], tag="r2c")
                    p.dma("sync", lambda g, occ=occ: g.dma_start(out=part_dram[occ], in_=r2c[:]),
                          ["R2C", f"part{occ}"], [f"part{occ}"], tag="r2spill")

    if tapt("r2", [16, 128, T]) is not None:
        p.dma("sync", lambda g: g.dma_start(out=tap_d["r2"], in_=part_dram),
              [f"part{i}" for i in range(16)], ["OUTD"])
    if tapt("r1fm", [128, NKC * T], BF16) is not None:
        p.dma("sync", lambda g: g.dma_start(out=tap_d["r1fm"], in_=r1fm), ["R1FM"], ["OUTD"])
    p.fence()
    for occ in range(16):
        p.dma("sync", lambda g, occ=occ: g.dma_start(out=r2c[:], in_=part_dram[occ]),
              [f"part{occ}", "R2C"], ["R2C"], tag="er2ld")
        p.s(lambda s: s.square(sqc[:], r2c[:]), ["R2C", "SQC"], ["SQC"], tag="esq")
        for jg in range(2):
            p.t(lambda t, jg=jg, occ=occ: t.matmul(pb[4 + jg][0:1, :], ones_col[:, 0:1],
                                                   r2c[:, jg * 512:(jg + 1) * 512],
                                                   start=(occ == 0), stop=(occ == 15)),
                ["R2C", "ones_col"], [PB[4 + jg]], tag="stsum")
            p.t(lambda t, jg=jg, occ=occ: t.matmul(pb[6 + jg][0:1, :], ones_col[:, 0:1],
                                                   sqc[:, jg * 512:(jg + 1) * 512],
                                                   start=(occ == 0), stop=(occ == 15)),
                ["SQC", "ones_col"], [PB[6 + jg]], tag="stsq")
    rowst = cv(AR5, 0, T, F32)[0:1, :]
    rowst2 = cv(AR5, 11264, T, F32)[0:1, :]
    p.v(lambda v: v.tensor_copy(rowst[0:1, 0:512], pb[4][0:1, :]), [PB[4], "rowst"], ["rowst"], tag="rs")
    p.v(lambda v: v.tensor_copy(rowst[0:1, 512:1024], pb[5][0:1, :]), [PB[5], "rowst"], ["rowst"], tag="rs")
    p.v(lambda v: v.tensor_copy(rowst2[0:1, 0:512], pb[6][0:1, :]), [PB[6], "rowst2"], ["rowst2"], tag="rs")
    p.v(lambda v: v.tensor_copy(rowst2[0:1, 512:1024], pb[7][0:1, :]), [PB[7], "rowst2"], ["rowst2"], tag="rs")
    p.v(lambda v: v.tensor_scalar(rowst[:], rowst[:], INV_DM, None, ALU.mult), ["rowst"], ["rowst"], tag="meanr")
    p.v(lambda v: v.tensor_scalar(rowst2[:], rowst2[:], INV_DM, None, ALU.mult), ["rowst2"], ["rowst2"], tag="m2r")
    for jg in range(2):
        p.v(lambda v, jg=jg: v.tensor_mul(pb[2][0:1, :], rowst[0:1, jg * 512:(jg + 1) * 512],
                                          rowst[0:1, jg * 512:(jg + 1) * 512]),
            ["rowst", PB[2]], [PB[2]], tag="mmr")
        p.v(lambda v, jg=jg: v.tensor_sub(rowst2[0:1, jg * 512:(jg + 1) * 512],
                                          rowst2[0:1, jg * 512:(jg + 1) * 512], pb[2][0:1, :]),
            ["rowst2", PB[2]], ["rowst2"], tag="varr")
    p.s(lambda s: s.activation(rowst2[:], rowst2[:], AF.Ln, bias=eps5[0:1, 0:1]),
        ["rowst2", "eps5"], ["rowst2"], tag="lnr")
    p.s(lambda s: s.activation(rowst2[:], rowst2[:], AF.Exp, scale=-0.5), ["rowst2"], ["rowst2"], tag="rstdr")
    for jg in range(2):
        p.t(lambda t, jg=jg: t.matmul(pb[0][:, :], ones_r32[:], rowst[0:1, jg * 512:(jg + 1) * 512],
                                      start=True, stop=True), ["rowst", "ones_r32"], [PB[0]], tag="bm")
        p.v(lambda v, jg=jg: v.tensor_copy(meanbc[:, jg * 512:(jg + 1) * 512], pb[0][:, :]),
            [PB[0], "MEANBC"], ["MEANBC"], tag="bmev")
        p.t(lambda t, jg=jg: t.matmul(pb[1][:, :], ones_r32[:], rowst2[0:1, jg * 512:(jg + 1) * 512],
                                      start=True, stop=True), ["rowst2", "ones_r32"], [PB[1]], tag="br")
        p.v(lambda v, jg=jg: v.tensor_copy(rstdbc[:, jg * 512:(jg + 1) * 512], pb[1][:, :]),
            [PB[1], "RSTDBC"], ["RSTDBC"], tag="brev")
    for occ in range(16):
        p.dma("sync", lambda g, occ=occ: g.dma_start(out=r2c[:], in_=part_dram[occ]),
              [f"part{occ}", "R2C"], ["R2C"], tag="r2ld")
        p.v(lambda v: v.tensor_sub(outc[:], r2c[:], meanbc[:]), ["R2C", "MEANBC", "OUTC"], ["OUTC"], tag="osub")
        p.v(lambda v: v.tensor_mul(outc[:], outc[:], rstdbc[:]), ["OUTC", "RSTDBC"], ["OUTC"], tag="omul")
        p.dma("sync", lambda g, occ=occ: g.dma_start(out=out_d[occ * 128:(occ + 1) * 128, :], in_=outc[:]),
              ["OUTC"], [f"OUT{occ}"], tag="ost")

    p.pool(lambda g: g.tensor_copy(stat[0:1, 0:1], stat[0:1, 0:1]),
           [f"OUT{i}" for i in range(16)] + ["OUTD", "stat"], ["fin"], tag="fin")

    p.emit(nc)
    return nc, sorted(tap_d.keys())



def _prep_shared(inputs):
    f32 = np.float32
    Wip = np.asarray(inputs["L0_attn_in_proj_w"], dtype=f32)
    WT = np.zeros((HID, NPAN * 512), dtype=f32)
    WT[:, :PROJ] = Wip.T
    wip = np.ascontiguousarray(WT.reshape(NKC, 128, NPAN, 512).transpose(2, 0, 1, 3))
    cw = np.asarray(inputs["L0_attn_conv_w"], dtype=f32)[:, 0, :]
    convd = np.zeros((34, 4, 128, 128), dtype=f32)
    idx = np.arange(128)
    cwr = cw.reshape(34, 128, 4)
    convd[:, :, idx, idx] = cwr.transpose(0, 2, 1)
    convb = np.ascontiguousarray(np.asarray(inputs["L0_attn_conv_b"], dtype=f32).reshape(34, 128).T)
    Wop = np.asarray(inputs["L0_attn_out_proj_w"], dtype=f32)
    wop = np.ascontiguousarray(Wop.T.reshape(32, 128, 4, 512).transpose(2, 0, 1, 3))
    Wg = np.asarray(inputs["L0_ffn_W_g"], dtype=f32)
    Wu = np.asarray(inputs["L0_ffn_W_u"], dtype=f32)
    Wd = np.asarray(inputs["L0_ffn_W_d"], dtype=f32)
    wg = np.ascontiguousarray(Wg.T.reshape(NKC, 128, 16, 512).transpose(2, 0, 1, 3))
    wu = np.ascontiguousarray(Wu.T.reshape(NKC, 128, 16, 512).transpose(2, 0, 1, 3))
    wd = np.ascontiguousarray(Wd.T.reshape(64, 128, 4, 512).transpose(2, 0, 1, 3))
    dtb = np.ascontiguousarray(np.tile(np.asarray(inputs["L0_attn_dt_bias"], dtype=f32), NTT)[None])
    alog = np.ascontiguousarray(np.tile(np.asarray(inputs["L0_attn_A_log"], dtype=f32), NTT)[None])
    drow = np.ascontiguousarray(np.asarray(inputs["L0_attn_D"], dtype=f32)[None])
    nw_t = np.ascontiguousarray(np.asarray(inputs["L0_attn_norm_weight"], dtype=f32).reshape(32, 128).T)
    ident = np.eye(128, dtype=f32)
    cumut = np.ascontiguousarray(np.triu(np.ones((128, 128), dtype=f32)))
    slo4 = np.ascontiguousarray(np.tile(
        (np.arange(128)[:, None] > np.arange(128)[None, :]).astype(f32), (1, 4)))
    negid = np.ascontiguousarray(30000.0 * np.eye(128, dtype=f32))
    e01 = np.ascontiguousarray(np.kron(np.eye(64, dtype=f32), np.ones((1, 64), dtype=f32)))
    return dict(wip=wip, convd=convd, convb=convb, wop=wop, wg=wg, wu=wu, wd=wd,
                dtb_t=dtb, alog_t=alog, d_row=drow, nw_t=nw_t,
                ident_in=ident, cumut_in=cumut, e01_in=e01,
                slo4_in=slo4, negid_in=negid)


def make_in_maps(inputs):
    shared = _prep_shared(inputs)
    x = np.asarray(inputs["L0_attn_hidden_states"], dtype=np.float32)
    maps = []
    for c in range(8):
        b, h = c // 2, c % 2
        xt = x[b, h * T:(h + 1) * T]
        if h == 0:
            ext = np.concatenate([np.zeros((3, HID), np.float32), xt], 0)
        else:
            ext = np.concatenate([x[b, h * T - 3:h * T], xt], 0)
        if h == 1:
            prev = np.concatenate([np.zeros((3, HID), np.float32), x[b, 0:T]], 0)
        else:
            prev = np.zeros((TE, HID), np.float32)
        m = dict(shared)
        m["x_fm"] = np.ascontiguousarray(ext.T)
        m["x_prev"] = np.ascontiguousarray(prev.T)
        m["x_tok"] = np.ascontiguousarray(xt)
        m["selm"] = np.full((128, 1), 1.0 if h == 1 else 0.0, np.float32)
        maps.append(m)
    return maps


def kernel(**inputs):
    nc, _ = build_nc()
    maps = make_in_maps(inputs)
    res = run_bass_kernel_spmd(nc, maps, core_ids=list(range(8)))
    out = np.empty((4, 2048, HID), np.float32)
    for c in range(8):
        b, h = c // 2, c % 2
        out[b, h * T:(h + 1) * T] = np.asarray(res.results[c]["out"], np.float32).T
    return out
